# version 11
# baseline (speedup 1.0000x reference)
import numpy as np
import ml_dtypes
import concourse.bass as bass
import concourse.mybir as mybir
from concourse.bass_utils import run_bass_kernel_spmd

F32 = mybir.dt.float32
BF16 = mybir.dt.bfloat16
U8 = mybir.dt.uint8
AF = mybir.ActivationFunctionType
AX = mybir.AxisListType
OP = mybir.AluOpType

NEG = -1.0e9
EPS = 1e-5
WAVELEN = 150000.0
ENGINES = ("sync", "scalar", "vector", "tensor", "gpsimd")


class CFG:
    def __init__(self, DM=2048, T=2048, H=32, HKV=8, E=16, I=1408, IS=5632,
                 TG=512, APT=256):
        self.DM, self.T, self.H, self.HKV, self.E, self.I, self.IS = DM, T, H, HKV, E, I, IS
        self.D = 64
        self.NCH = DM // 128
        self.GRP = H // HKV
        self.QCH = H * 64 // 128
        self.KVCH = HKV * 64 // 128
        self.ICN = I // 128
        self.ISCN = IS // 128
        self.NSHU = self.ISCN // self.ICN
        assert self.ISCN % self.ICN == 0
        self.HALF = T // 2
        self.TG = TG
        self.NTG = self.HALF // TG
        assert self.NTG == 2
        self.APT = APT
        self.NPT = T // APT
        self.SUBT = APT // 128
        self.NT128 = T // 128
        self.NSEL = E + 1
        self.NU = E + self.NSHU
        self.SCALE = self.D ** (-0.25)
        self.h_chunk = [((h // self.GRP) // 2) * self.GRP + (h % self.GRP)
                        for h in range(H)]
        self.h_base = [64 * ((h // self.GRP) % 2) for h in range(H)]
        perm = np.zeros(H * 64, dtype=np.int64)
        for h in range(H):
            dst = self.h_chunk[h] * 128 + self.h_base[h]
            perm[dst:dst + 64] = np.arange(h * 64, (h + 1) * 64)
        self.q_perm = perm


CFG_FULL = CFG()
CFG_MINI = CFG(DM=256, T=512, H=4, HKV=2, E=8, I=256, IS=512, TG=128, APT=128)


def make_consts(cfg):
    bf = ml_dtypes.bfloat16
    identb = np.eye(128, dtype=np.float32).astype(bf)
    identf = np.eye(128, dtype=np.float32)
    pswap = np.zeros((128, 128), dtype=np.float32)
    for i in range(128):
        pswap[i, i ^ 1] = 1.0
    ones = np.ones((128, 128), dtype=np.float32)
    a = np.arange(128)[:, None]
    b = np.arange(128)[None, :]
    masks = np.zeros((128, 384), dtype=np.float32)
    masks[:, 0:128] = np.where(a <= b, 0.0, NEG)
    masks[:, 256:384] = np.where(b <= a, 0.0, NEG)
    nsel = cfg.NSEL
    sel = np.zeros((nsel, nsel * 128), dtype=np.float32)
    for e in range(nsel):
        sel[e, e * 128:(e + 1) * 128] = 1.0
    sel = sel.astype(bf)
    half = cfg.D // 2
    inv_freq = (1.0 / WAVELEN ** (np.arange(half, dtype=np.float32) * 2.0 / cfg.D)).astype(np.float32)
    t = np.arange(cfg.T, dtype=np.float32)
    freqs = t[:, None] * inv_freq[None, :]
    cosv = np.cos(freqs).astype(np.float32)
    sinv = np.sin(freqs).astype(np.float32)
    p = np.arange(128)
    fidx = (p % 64) // 2
    sign = np.where(p % 2 == 0, -1.0, 1.0).astype(np.float32)
    ctab = np.ascontiguousarray((cosv[:, fidx].T * cfg.SCALE).astype(np.float32))
    stab = np.ascontiguousarray((sinv[:, fidx].T * cfg.SCALE * sign[:, None]).astype(np.float32))
    iota = np.arange(128, dtype=np.float32)
    mtab = np.zeros((128, 8), dtype=np.float32)
    mtab[:, 1] = iota
    mtab[:, 2] = iota + 257.0
    mtab[:, 3] = iota + 129.0
    mtab[:, 4] = 256.0
    return dict(identb=identb, identf=identf, pswap=pswap, ones=ones,
                masks=masks, sel=sel, ctab=ctab, stab=stab, mtab=mtab)


class Bld:

    def __init__(self, sems):
        self.sems = sems
        self.cur = None
        self.eng = None
        self.cnt = None

    def begin(self, name, eng):
        self.cur = name
        self.eng = eng
        self.cnt = {k: 0 for k in self.sems}

    def _waits(self, waits):
        w = {}
        for s, v in waits:
            if v:
                w[s] = max(w.get(s, 0), v)
        for s, v in w.items():
            self.eng.wait_ge(self.sems[s], v)

    def dma(self, q, sem, out, in_, waits=(), **kw):
        self.cnt[sem] += 16
        if self.cur == q:
            self._waits(waits)
            self.eng.dma_start(out=out, in_=in_, **kw).then_inc(self.sems[sem], 16)
        return self.cnt[sem]

    def dmat(self, q, sem, out, in_, waits=()):
        self.cnt[sem] += 16
        if self.cur == q:
            self._waits(waits)
            self.eng.dma_start_transpose(out, in_).then_inc(self.sems[sem], 16)
        return self.cnt[sem]

    def op(self, engine, fn, inc=None, waits=()):
        tok = None
        if inc is not None:
            self.cnt[inc] += 1
            tok = self.cnt[inc]
        if self.cur == engine:
            self._waits(waits)
            inst = fn(self.eng)
            if inc is not None:
                assert inst is not None
                inst.then_inc(self.sems[inc], 1)
        return tok

    def wait(self, engine, waits):
        if self.cur == engine:
            self._waits(waits)

    def fence(self, engines=ENGINES):
        snap = [(s, v) for s, v in self.cnt.items() if v > 0]
        for e in engines:
            self.wait(e, snap)


def build_program(cfg, phases=("cast", "ffnA", "attn", "ffnB")):
    nc = bass.Bass("TRN2", target_bir_lowering=False)
    phases = set(phases)
    c = cfg
    bf2, f4 = 2, 4

    def din(name, shape, dt=F32):
        return nc.dram_tensor(name, shape, dt, kind="ExternalInput").ap()

    xin = din("xin", [c.DM, c.T])
    gwf = din("gwf", [c.E * c.ICN, 128, c.NCH * 128])
    uwf = din("uwf", [c.E * c.ICN, 128, c.NCH * 128])
    dwf = din("dwf", [c.E * c.NCH, 128, c.ICN * 128])
    sgw = din("sgw", [c.ISCN, 128, c.NCH * 128])
    suw = din("suw", [c.ISCN, 128, c.NCH * 128])
    sdw = din("sdw", [c.NCH, 128, c.ISCN * 128])
    rw = din("rw", [c.E, c.DM])
    sew = din("sew", [1, c.DM])
    snk = din("snk", [1, c.H])
    wqd = din("wq", [c.QCH, 128, c.NCH * 128])
    wkd = din("wk", [c.KVCH, 128, c.NCH * 128])
    wvd = din("wv", [c.KVCH, 128, c.NCH * 128])
    wod = din("wo", [c.NCH, 128, c.QCH * 128])
    bqd = din("bq", [128, c.QCH])
    bkvd = din("bkv", [128, 2 * c.KVCH])
    bod = din("bo", [128, c.NCH])
    identbd = din("identb", [128, 128], BF16)
    identfd = din("identf", [128, 128])
    pswapd = din("pswap", [128, 128])
    onesd = din("ones", [128, 128])
    masksd = din("masks", [128, 384])
    seld = din("sel", [c.NSEL, c.NSEL * 128], BF16)
    ctabd = din("ctab", [128, c.T])
    stabd = din("stab", [128, c.T])
    mtabd = din("mtab", [128, 8])

    xout = nc.dram_tensor("xout", [c.DM, c.T], F32, kind="ExternalOutput").ap()

    gwt_c = nc.dram_tensor("gwt_c", [c.E * c.ICN, 128, c.DM], BF16, kind="Internal").ap()
    uwt_c = nc.dram_tensor("uwt_c", [c.E * c.ICN, 128, c.DM], BF16, kind="Internal").ap()
    sgwt_c = nc.dram_tensor("sgwt_c", [c.ISCN, 128, c.DM], BF16, kind="Internal").ap()
    suwt_c = nc.dram_tensor("suwt_c", [c.ISCN, 128, c.DM], BF16, kind="Internal").ap()
    dwt_c = nc.dram_tensor("dwt_c", [c.E * c.NCH, 128, c.I], BF16, kind="Internal").ap()
    sdwt_c = nc.dram_tensor("sdwt_c", [c.NCH, 128, c.IS], BF16, kind="Internal").ap()
    wq_c = nc.dram_tensor("wq_c", [c.QCH, 128, c.NCH * 128], BF16, kind="Internal").ap()
    wk_c = nc.dram_tensor("wk_c", [c.KVCH, 128, c.NCH * 128], BF16, kind="Internal").ap()
    wv_c = nc.dram_tensor("wv_c", [c.KVCH, 128, c.NCH * 128], BF16, kind="Internal").ap()
    wo_c = nc.dram_tensor("wo_c", [c.NCH, 128, c.QCH * 128], BF16, kind="Internal").ap()
    park = nc.dram_tensor("park", [2, c.NCH, 128, c.HALF], F32, kind="Internal").ap()

    RB = c.NCH * c.HALF * f4
    R1, R2, P0 = 0, RB, 2 * RB

    def mklay(items, base):
        lay, off = {}, base
        for name, sz in items:
            sz = (sz + 63) & ~63
            lay[name] = (off, sz)
            off += sz
        return lay, off

    ffn_r2_items = [
        ("YH", c.NCH * c.HALF * bf2),
        ("HBUF", c.ICN * c.HALF * bf2),
        ("HS", c.HALF * bf2),
        ("UB", c.HALF * bf2),
    ]
    ffn_r2, ffn2_end = mklay(ffn_r2_items, R2)

    ffn_pool, ffnp_end = mklay([
        ("GSL", 2 * c.DM * bf2),
        ("USL", 2 * c.DM * bf2),
        ("DSL", 2 * c.ICN * 128 * bf2),
        ("WBC", 2 * c.HALF * f4),
        ("MBC", c.TG * f4), ("RBC", c.TG * f4),
        ("YS", 2 * c.TG * f4),
        ("YSF", 2 * c.TG * f4),
        ("LGT", c.TG * f4),
        ("LROW", (c.TG // 128) * 32 * f4),
        ("WST", (c.TG // 128) * (c.NSEL + 3) * bf2),
        ("T8", 8 * f4), ("NMr", 64), ("DSr", 64), ("RCr", 64),
        ("PEX", c.E * f4), ("MSK", c.E * f4),
    ], P0)
    if ffn2_end > P0:
        ffn_r2, ffnp_end = mklay(ffn_r2_items, ffnp_end)

    castw = c.NCH * 128 * f4
    cast_pool, cast_end = mklay([
        ("CINF", 2 * castw),
        ("CINB", 2 * (castw // 2)),
    ], ffnp_end)
    setup_items = [
        ("RWL", c.DM * f4),
        ("SEW", c.DM * f4),
        ("SNKL", 64 * f4),
    ]
    setup_pool, setup_end = mklay(setup_items, R2)
    if setup_end > P0:
        setup_pool, setup_end = mklay(setup_items, cast_end)

    attn_pool, attn_end = mklay([
        ("KFM", c.KVCH * c.T * bf2),
        ("VTM", c.NT128 * c.HKV * 64 * bf2),
        ("QT", c.QCH * c.APT * bf2),
        ("YOT", c.NCH * c.APT * bf2),
        ("AWB", 2 * c.NCH * 128 * bf2),
        ("CSLc", c.APT * f4), ("CSLs", c.APT * f4),
        ("KB", c.APT * f4), ("KB2", c.APT * f4),
        ("VTMP", c.APT * bf2),
        ("PRN", 3 * 384 * bf2), ("PRT", 2 * 384 * bf2),
        ("ANM", 64), ("ADS", 64), ("ARC", 64), ("AES", 64),
        ("XSQA", 2 * c.APT * f4),
    ], P0)

    pool_end = max(ffnp_end, cast_end, attn_end, setup_end)
    assert cast_end <= P0 + 71000, cast_end
    const_lay, const_end = mklay([
        ("identb", 128 * bf2), ("identf", 128 * f4), ("pswap", 128 * f4),
        ("ones", 128 * f4), ("masks", 384 * f4), ("sel", c.NSEL * 128 * bf2),
        ("biasq", c.QCH * f4), ("biaskv", 2 * c.KVCH * f4), ("biaso", c.NCH * f4),
        ("sinkrep", 2 * c.H * f4), ("rwseT", c.NCH * c.NSEL * f4),
        ("wTs", c.HALF * bf2), ("mtab", 8 * f4),
    ], pool_end)
    ARENA = const_end
    assert ARENA <= 212800, f"SBUF over budget: {ARENA}"

    lay = {}
    for d in (ffn_r2, ffn_pool, cast_pool, setup_pool, attn_pool, const_lay):
        lay.update(d)

    SEM_NAMES = ["sLD", "sPK",
                 "sGL0", "sGL1", "sTR0", "sTR1", "sST0", "sST1",
                 "sGU0", "sGU1", "sDS0", "sDS1", "sAW0", "sAW1", "sCS",
                 "sPE", "sDV", "sAC"]
    from contextlib import ExitStack
    with ExitStack() as stack:
        arena = stack.enter_context(nc.sbuf_tensor("arena", [128, ARENA], U8))
        ps0 = stack.enter_context(nc.psum_tensor("ps0", [128, 512], F32))
        ps1 = stack.enter_context(nc.psum_tensor("ps1", [128, 512], F32))
        ps2 = stack.enter_context(nc.psum_tensor("ps2", [128, 512], F32))
        ps3 = stack.enter_context(nc.psum_tensor("ps3", [128, 512], F32))
        ps4 = stack.enter_context(nc.psum_tensor("ps4", [128, 512], F32))
        ps5 = stack.enter_context(nc.psum_tensor("ps5", [128, 512], F32))
        ps6 = stack.enter_context(nc.psum_tensor("ps6", [128, 512], F32))
        psb = stack.enter_context(nc.psum_tensor("psb", [128, 768], BF16))
        sems = {nm: stack.enter_context(nc.semaphore(nm)) for nm in SEM_NAMES}
        block = stack.enter_context(nc.Block())
        PS = [ps0, ps1, ps2, ps3, ps4, ps5, ps6]
        B = Bld(sems)

        def cv(name, dtype, parts=128, off_b=0, nelem=None):
            off, sz = lay[name]
            dsz = {F32: 4, BF16: 2, U8: 1}[dtype]
            nb = sz - off_b if nelem is None else nelem * dsz
            return arena[0:parts, off + off_b: off + off_b + nb].bitcast(dtype)

        def xsl(base, ch, t0g, n):
            off = base + (ch * c.HALF + (t0g % c.HALF)) * f4
            return arena[:, off: off + n * f4].bitcast(F32)

        def schedule():
            cnt = B.cnt
            idb = cv("identb", BF16)
            idf = cv("identf", F32)
            psw = cv("pswap", F32)
            onesap = cv("ones", F32)
            masks = cv("masks", F32)
            selap = cv("sel", BF16, parts=c.NSEL)
            biasq = cv("biasq", F32)
            biaskv = cv("biaskv", F32)
            biaso = cv("biaso", F32)
            sinkrep = cv("sinkrep", F32)
            rwseT = cv("rwseT", F32)
            wTs = cv("wTs", BF16, parts=c.NSEL)

            tcl = []
            for nm, dt_, src in (("identb", BF16, identbd), ("identf", F32, identfd),
                                 ("pswap", F32, pswapd), ("ones", F32, onesd),
                                 ("masks", F32, masksd)):
                tcl.append(B.dma("sync", "sLD", cv(nm, dt_), src[:, :]))
            tcl.append(B.dma("sync", "sLD", selap, seld[:, :]))
            tcl.append(B.dma("sync", "sLD", cv("biasq", F32, nelem=c.QCH), bqd[:, :]))
            tcl.append(B.dma("sync", "sLD", cv("biaskv", F32, nelem=2 * c.KVCH), bkvd[:, :]))
            tcl.append(B.dma("sync", "sLD", cv("biaso", F32, nelem=c.NCH), bod[:, :]))
            tcl.append(B.dma("sync", "sLD", cv("mtab", F32, nelem=8), mtabd[:, :]))
            for ch in range(c.NCH):
                tcl.append(B.dma("sync", "sLD", xsl(R1, ch, 0, c.HALF),
                                 xin[ch * 128:(ch + 1) * 128, 0:c.HALF]))
            t_rw = B.dma("sync", "sLD", cv("RWL", F32, parts=c.E), rw[:, :])
            t_se = B.dma("sync", "sLD", cv("SEW", F32, parts=1), sew[:, :])
            t_sk = B.dma("sync", "sLD", cv("SNKL", F32, parts=1, nelem=c.H), snk[:, :])
            tok_setup = cnt["sLD"]
            rwl = cv("RWL", F32, parts=c.E)
            sewr = cv("SEW", F32, parts=1)
            snkl = cv("SNKL", F32, parts=1, nelem=c.H)

            prev_ev = 0
            for j in range(c.NCH):
                tmm = B.op("tensor", lambda e, j=j: (
                    e.transpose(ps5[0:128, 0:c.E], rwl[0:c.E, j * 128:(j + 1) * 128],
                                idf[0:c.E, 0:c.E]),
                    e.transpose(ps6[0:128, 0:1], sewr[0:1, j * 128:(j + 1) * 128],
                                idf[0:1, 0:1]))[-1],
                    inc="sPE",
                    waits=[("sLD", tok_setup), ("sDV", prev_ev)])
                prev_ev = B.op("vector", lambda e, j=j: (
                    e.tensor_copy(rwseT[:, j * c.NSEL: j * c.NSEL + c.E], ps5[0:128, 0:c.E]),
                    e.tensor_copy(rwseT[:, j * c.NSEL + c.E: (j + 1) * c.NSEL],
                                  ps6[0:128, 0:1]))[-1],
                    inc="sDV", waits=[("sPE", tmm)])
            tmm = B.op("tensor", lambda e: e.matmul(
                ps5[0:128, 0:c.H], onesap[0:1, 0:128], snkl[0:1, 0:c.H],
                start=True, stop=True),
                inc="sPE", waits=[("sLD", tok_setup), ("sDV", prev_ev)])
            B.op("vector", lambda e: (
                e.tensor_copy(sinkrep[:, 0:c.H], ps5[0:128, 0:c.H]),
                e.tensor_scalar_mul(sinkrep[:, c.H:2 * c.H], ps5[0:128, 0:c.H], -1.0))[-1],
                inc="sDV", waits=[("sPE", tmm)])
            tok_setup_dv = cnt["sDV"]

            CW = c.NCH * 128
            CINF = [cv("CINF", F32, off_b=0, nelem=CW),
                    cv("CINF", F32, off_b=castw, nelem=CW)]
            CINB = [cv("CINB", BF16, off_b=0, nelem=CW),
                    cv("CINB", BF16, off_b=castw // 2, nelem=CW)]
            jobs = []
            for e_ in range(c.E):
                for ic in range(c.ICN):
                    sidx = e_ * c.ICN + ic
                    jobs.append((gwf[sidx], CW, gwt_c[sidx][:, :], ("gw", sidx)))
                    jobs.append((uwf[sidx], CW, uwt_c[sidx][:, :], ("uw", sidx)))
                for d_ in range(c.NCH):
                    sidx = e_ * c.NCH + d_
                    jobs.append((dwf[sidx], c.ICN * 128, dwt_c[sidx][:, :],
                                 ("dw", sidx)))
            for j_ in range(c.NSHU):
                for ic in range(c.ICN):
                    sidx = j_ * c.ICN + ic
                    jobs.append((sgw[sidx], CW, sgwt_c[sidx][:, :], ("sgw", sidx)))
                    jobs.append((suw[sidx], CW, suwt_c[sidx][:, :], ("suw", sidx)))
            IW = c.ICN * 128
            for q_ in range(c.NSHU):
                for d_ in range(c.NCH):
                    jobs.append((sdw[d_][:, q_ * IW:(q_ + 1) * IW], IW,
                                 sdwt_c[d_][:, q_ * IW:(q_ + 1) * IW],
                                 ("sdw", d_ * c.NSHU + q_)))
            for j_ in range(c.QCH):
                jobs.append((wqd[j_], CW, wq_c[j_][:, :], ("wq", j_)))
            for j_ in range(c.KVCH):
                jobs.append((wkd[j_], CW, wk_c[j_][:, :], ("wk", j_)))
                jobs.append((wvd[j_], CW, wv_c[j_][:, :], ("wv", j_)))
            for j_ in range(c.NCH):
                jobs.append((wod[j_], c.QCH * 128, wo_c[j_][:, :], ("wo", j_)))

            conv_tok = [0, 0]
            store_tok = [0, 0]
            stok = {}
            if "cast" not in phases:
                jobs = []
            cast_cursor = [0]

            def emit_cast(n):
                lo = cast_cursor[0]
                hi = min(lo + n, len(jobs))
                cast_cursor[0] = hi
                for si in range(lo, hi):
                    src, w, img, key = jobs[si]
                    p = si % 2
                    tl = B.dma("sync", f"sTR{p}", CINF[p][:, 0:w], src,
                               waits=[("sAC", conv_tok[p])])
                    tc = B.op("scalar", lambda e, p=p, w=w: e.activation(
                        CINB[p][:, 0:w], CINF[p][:, 0:w], AF.Identity),
                        inc="sAC",
                        waits=[(f"sTR{p}", tl), (f"sST{p}", store_tok[p])])
                    conv_tok[p] = tc
                    ts = B.dma("scalar", f"sST{p}", img, CINB[p][:, 0:w],
                               waits=[("sAC", tc)])
                    store_tok[p] = ts
                    stok[key] = (f"sST{p}", ts)

            def img_wait(key):
                if key in stok:
                    return [stok[key]]
                return []

            YH = cv("YH", BF16)
            HBUF = cv("HBUF", BF16)
            HS = cv("HS", BF16)
            UB = cv("UB", BF16)
            GSL = [cv("GSL", BF16, off_b=0, nelem=c.DM),
                   cv("GSL", BF16, off_b=c.DM * bf2, nelem=c.DM)]
            USL = [cv("USL", BF16, off_b=0, nelem=c.DM),
                   cv("USL", BF16, off_b=c.DM * bf2, nelem=c.DM)]
            DSL = [cv("DSL", BF16, off_b=0, nelem=c.ICN * 128),
                   cv("DSL", BF16, off_b=c.ICN * 128 * bf2, nelem=c.ICN * 128)]
            WBC = [cv("WBC", F32, off_b=0, nelem=c.HALF),
                   cv("WBC", F32, off_b=c.HALF * f4, nelem=c.HALF)]
            MBC = cv("MBC", F32)
            RBC = cv("RBC", F32)
            XSQ = [cv("YS", F32, off_b=0, nelem=c.TG),
                   cv("YS", F32, off_b=c.TG * f4, nelem=c.TG)]
            SRA = cv("YS", F32, parts=1, off_b=0, nelem=c.TG)
            SRB = cv("YS", F32, parts=1, off_b=c.TG * f4, nelem=c.TG)
            YS = [cv("YS", F32, off_b=0, nelem=c.TG),
                  cv("YS", F32, off_b=c.TG * f4, nelem=c.TG)]
            YSF = [cv("YSF", F32, off_b=0, nelem=c.TG),
                   cv("YSF", F32, off_b=c.TG * f4, nelem=c.TG)]
            LGT = cv("LGT", F32, parts=c.NSEL)
            LROW = cv("LROW", F32)
            WSTb = cv("WST", BF16)
            T8 = cv("T8", F32, nelem=8)
            NMr = cv("NMr", F32, nelem=1)
            DSr = cv("DSr", F32, nelem=1)
            RCr = cv("RCr", F32, nelem=1)
            PEXr = cv("PEX", F32, nelem=c.E)
            MSKr = cv("MSK", F32, nelem=c.E)

            def ln_router(base, half, with_router, start_waits=()):
                wts_toks = []
                yread = [0, 0]
                for tg in range(c.NTG):
                    t0 = half * c.HALF + tg * c.TG
                    sq_t = [0, 0]
                    mm_t = [0, 0]
                    last_mm = 0
                    for ch in range(c.NCH):
                        par = ch % 2
                        ta = B.op("scalar", lambda e, ch=ch, par=par: e.activation(
                            XSQ[par], xsl(base, ch, t0, c.TG), AF.Square),
                            inc="sAC", waits=[("sPE", mm_t[par]),
                                              ("sDV", yread[par])] + list(start_waits))
                        last_mm = B.op("tensor", lambda e, ch=ch, par=par: (
                            e.matmul(ps5[0:1, 0:c.TG], onesap[:, 0:1],
                                     xsl(base, ch, t0, c.TG),
                                     start=(ch == 0), stop=(ch == c.NCH - 1)),
                            e.matmul(ps6[0:1, 0:c.TG], onesap[:, 0:1], XSQ[par],
                                     start=(ch == 0), stop=(ch == c.NCH - 1)))[-1],
                            inc="sPE", waits=[("sAC", ta)] + list(start_waits))
                        mm_t[par] = last_mm
                    tv = B.op("vector", lambda e: (
                        e.tensor_scalar_mul(SRB, ps6[0:1, 0:c.TG], 1.0 / c.DM),
                        e.tensor_scalar_mul(SRA, ps5[0:1, 0:c.TG], 1.0 / c.DM))[-1],
                        inc="sDV", waits=[("sPE", last_mm), ("sDV", max(yread))])
                    ta = B.op("scalar", lambda e: e.activation(
                        ps5[0:1, 0:c.TG], SRA, AF.Square),
                        inc="sAC", waits=[("sDV", tv)])
                    tv = B.op("vector", lambda e: e.tensor_sub(
                        SRB, SRB, ps5[0:1, 0:c.TG]),
                        inc="sDV", waits=[("sAC", ta)])
                    tv = B.op("vector", lambda e: e.tensor_scalar_add(SRB, SRB, EPS),
                              inc="sDV", waits=[("sDV", tv)])
                    ta = B.op("scalar", lambda e: e.activation(SRB, SRB, AF.Sqrt),
                              inc="sAC", waits=[("sDV", tv)])
                    tv = B.op("vector", lambda e: e.reciprocal(SRB, SRB),
                              inc="sDV", waits=[("sAC", ta)])
                    tb = B.op("tensor", lambda e: (
                        e.matmul(ps5[0:128, 0:c.TG], onesap[0:1, 0:128], SRA,
                                 start=True, stop=True),
                        e.matmul(ps6[0:128, 0:c.TG], onesap[0:1, 0:128], SRB,
                                 start=True, stop=True))[-1],
                        inc="sPE", waits=[("sDV", tv)])
                    tbc = B.op("vector", lambda e: (
                        e.tensor_copy(MBC, ps5[0:128, 0:c.TG]),
                        e.tensor_copy(RBC, ps6[0:128, 0:c.TG]))[-1],
                        inc="sDV", waits=[("sPE", tb)])
                    rt = [0, 0]
                    last_r = 0
                    tv_tok = [0, 0]
                    for ch in range(c.NCH):
                        par = ch % 2
                        tv1 = B.op("vector", lambda e, ch=ch, par=par: e.tensor_sub(
                            YS[par], xsl(base, ch, t0, c.TG), MBC),
                            inc="sDV", waits=[("sDV", tbc), ("sPE", rt[par])])
                        tv2 = B.op("vector", lambda e, par=par: e.tensor_mul(
                            YSF[par], YS[par], RBC),
                            inc="sDV", waits=[("sDV", tv1)])
                        yread[par] = tv2
                        tv3 = B.op("vector", lambda e, ch=ch, par=par: e.tensor_copy(
                            YH[:, ch * c.HALF + tg * c.TG: ch * c.HALF + (tg + 1) * c.TG],
                            YSF[par]),
                            inc="sDV", waits=[("sDV", tv2)])
                        tv_tok[par] = tv3
                        if with_router:
                            last_r = B.op("tensor", lambda e, ch=ch, par=par: e.matmul(
                                ps4[0:c.NSEL, 0:c.TG],
                                rwseT[:, ch * c.NSEL:(ch + 1) * c.NSEL], YSF[par],
                                start=(ch == 0), stop=(ch == c.NCH - 1)),
                                inc="sPE", waits=[("sDV", tv2)])
                            rt[par] = last_r
                    if not with_router:
                        continue
                    te = B.op("vector", lambda e: e.tensor_copy(
                        LGT[0:c.NSEL, 0:c.TG], ps4[0:c.NSEL, 0:c.TG]),
                        inc="sDV", waits=[("sPE", last_r)])
                    prev_lrow = 0
                    for s in range(c.TG // 128):
                        tt = B.op("tensor", lambda e, s=s: e.transpose(
                            ps5[0:128, 0:c.NSEL], LGT[0:c.NSEL, s * 128:(s + 1) * 128],
                            idf[0:c.NSEL, 0:c.NSEL]),
                            inc="sPE", waits=[("sDV", te), ("sDV", prev_lrow)])
                        tl = B.op("vector", lambda e, s=s: e.tensor_copy(
                            LROW[:, s * 32: s * 32 + c.NSEL], ps5[0:128, 0:c.NSEL]),
                            inc="sDV", waits=[("sPE", tt)])
                        prev_lrow = tl
                        t8t = B.op("vector", lambda e, s=s: e.max(
                            out=T8, in_=LROW[:, s * 32: s * 32 + c.E]),
                            inc="sDV", waits=[("sDV", tl)])
                        tnm = B.op("vector", lambda e: e.tensor_scalar_mul(
                            NMr, T8[:, 0:1], -1.0),
                            inc="sDV", waits=[("sDV", t8t)])
                        tex = B.op("scalar", lambda e, s=s: e.activation(
                            PEXr, LROW[:, s * 32: s * 32 + c.E], AF.Exp,
                            bias=NMr, accum_out=DSr),
                            inc="sAC", waits=[("sDV", tnm)])
                        trc = B.op("vector", lambda e: e.reciprocal(RCr, DSr),
                                   inc="sDV", waits=[("sAC", tex)])
                        tmk = B.op("vector", lambda e, s=s: e.tensor_single_scalar(
                            MSKr, LROW[:, s * 32: s * 32 + c.E], T8[:, 3:4], OP.is_ge),
                            inc="sDV", waits=[("sDV", t8t)])
                        tw1 = B.op("vector", lambda e: e.tensor_mul(PEXr, PEXr, MSKr),
                                   inc="sDV", waits=[("sDV", tmk), ("sDV", trc)])
                        tw2 = B.op("vector", lambda e, s=s: e.tensor_scalar(
                            WSTb[:, s * (c.NSEL + 3): s * (c.NSEL + 3) + c.E],
                            PEXr, RCr, 0.5, OP.mult, OP.mult),
                            inc="sDV", waits=[("sDV", tw1)])
                        tsg = B.op("scalar", lambda e, s=s: e.activation(
                            DSr, LROW[:, s * 32 + c.E: s * 32 + c.E + 1], AF.Sigmoid),
                            inc="sAC", waits=[("sDV", trc)])
                        tsg2 = B.op("vector", lambda e, s=s: e.tensor_scalar_mul(
                            WSTb[:, s * (c.NSEL + 3) + c.E: s * (c.NSEL + 3) + c.NSEL],
                            DSr, 0.5),
                            inc="sDV", waits=[("sAC", tsg)])
                        ttb = B.op("tensor", lambda e, s=s: e.transpose(
                            psb[0:c.NSEL, s * 128:(s + 1) * 128],
                            WSTb[:, s * (c.NSEL + 3): s * (c.NSEL + 3) + c.NSEL], idb),
                            inc="sPE", waits=[("sDV", tw2), ("sDV", tsg2)])
                        twt = B.op("vector", lambda e, s=s, tg=tg: e.tensor_copy(
                            wTs[0:c.NSEL, tg * c.TG + s * 128: tg * c.TG + (s + 1) * 128],
                            psb[0:c.NSEL, s * 128:(s + 1) * 128]),
                            inc="sDV", waits=[("sPE", ttb)])
                        wts_toks.append(twt)
                return wts_toks

            def ffn_half(base, half, start_waits=(), unit_hook=None,
                         step_hook=None):
                wts_toks = ln_router(base, half, True, start_waits)
                tok_wts = max(wts_toks)
                guA = {}
                guU = {}
                silu_t = {}
                ub_t = {}
                h_t = {}
                dsl_stop = {}
                down_grp = []
                grpc = 0
                last_down_stop = 0
                last_wbc_mm = 0
                for u in range(c.NU):
                    moe = u < c.E
                    erow = u if moe else c.E
                    up = u % 2
                    wbc_ev = []
                    for tgi in range(c.NTG):
                        bank = PS[4 + (grpc % 3)]
                        guard = down_grp[grpc - 3] if grpc >= 3 else 0
                        tw = B.op("tensor", lambda e, erow=erow, tgi=tgi, bank=bank: e.matmul(
                            bank[0:128, 0:c.TG],
                            selap[0:c.NSEL, erow * 128:(erow + 1) * 128],
                            wTs[0:c.NSEL, tgi * c.TG:(tgi + 1) * c.TG],
                            start=True, stop=True),
                            inc="sPE", waits=[("sDV", guard), ("sDV", tok_wts)])
                        last_wbc_mm = tw
                        tv = B.op("vector", lambda e, tgi=tgi, up=up, bank=bank: e.tensor_copy(
                            WBC[up][:, tgi * c.TG:(tgi + 1) * c.TG], bank[0:128, 0:c.TG]),
                            inc="sDV", waits=[("sPE", tw)])
                        down_grp.append(tv)
                        wbc_ev.append(tv)
                        grpc += 1
                    tok_wbc = max(wbc_ev)
                    for ic in range(c.ICN):
                        gsi = u * c.ICN + ic
                        p = gsi % 2
                        if moe:
                            img_g = gwt_c[u * c.ICN + ic]
                            img_u = uwt_c[u * c.ICN + ic]
                            kg = ("gw", u * c.ICN + ic)
                            ku = ("uw", u * c.ICN + ic)
                        else:
                            img_g = sgwt_c[(u - c.E) * c.ICN + ic]
                            img_u = suwt_c[(u - c.E) * c.ICN + ic]
                            kg = ("sgw", (u - c.E) * c.ICN + ic)
                            ku = ("suw", (u - c.E) * c.ICN + ic)
                        free_tok = guU.get(gsi - 2, 0)
                        B.dma("sync", f"sGU{p}", GSL[p], img_g[:, :],
                              waits=[("sPE", free_tok)] + img_wait(kg))
                        wl = B.dma("sync", f"sGU{p}", USL[p], img_u[:, :],
                                   waits=img_wait(ku))

                        def g_mms(e, p=p, half=half):
                            last = None
                            for j in range(c.NCH):
                                lhs = GSL[p][:, j * 128:(j + 1) * 128]
                                for tgi in range(c.NTG):
                                    last = e.matmul(
                                        PS[tgi][0:128, 0:c.TG], lhs,
                                        YH[:, j * c.HALF + tgi * c.TG: j * c.HALF + (tgi + 1) * c.TG],
                                        start=(j == 0), stop=(j == c.NCH - 1))
                            return last

                        def u_mms(e, p=p, half=half):
                            last = None
                            for j in range(c.NCH):
                                lhs = USL[p][:, j * 128:(j + 1) * 128]
                                for tgi in range(c.NTG):
                                    last = e.matmul(
                                        PS[2 + tgi][0:128, 0:c.TG], lhs,
                                        YH[:, j * c.HALF + tgi * c.TG: j * c.HALF + (tgi + 1) * c.TG],
                                        start=(j == 0), stop=(j == c.NCH - 1))
                            return last

                        gA = B.op("tensor", g_mms, inc="sPE",
                                  waits=[(f"sGU{p}", wl), ("sAC", silu_t.get(gsi - 1, 0))])
                        uA = B.op("tensor", u_mms, inc="sPE",
                                  waits=[("sDV", ub_t.get(gsi - 1, 0))])
                        guA[gsi], guU[gsi] = gA, uA
                        ts_ = B.op("scalar", lambda e: (
                            e.activation(HS[:, 0:c.TG], ps0[0:128, 0:c.TG], AF.Silu),
                            e.activation(HS[:, c.TG:2 * c.TG], ps1[0:128, 0:c.TG], AF.Silu))[-1],
                            inc="sAC",
                            waits=[("sPE", gA), ("sDV", h_t.get(gsi - 1, 0))])
                        tu_ = B.op("vector", lambda e, up=up: (
                            e.tensor_mul(UB[:, 0:c.TG], ps2[0:128, 0:c.TG], WBC[up][:, 0:c.TG]),
                            e.tensor_mul(UB[:, c.TG:2 * c.TG], ps3[0:128, 0:c.TG],
                                         WBC[up][:, c.TG:2 * c.TG]))[-1],
                            inc="sDV",
                            waits=[("sPE", uA), ("sDV", max(h_t.get(gsi - 1, 0), tok_wbc)),
                                   ("sPE", last_down_stop)])
                        th_ = B.op("vector", lambda e, ic=ic: e.tensor_mul(
                            HBUF[:, ic * c.HALF:(ic + 1) * c.HALF], HS[:, 0:c.HALF],
                            UB[:, 0:c.HALF]),
                            inc="sDV",
                            waits=[("sAC", ts_), ("sDV", tu_), ("sPE", last_down_stop)])
                        silu_t[gsi], ub_t[gsi], h_t[gsi] = ts_, tu_, th_
                        if step_hook is not None:
                            step_hook()
                    tok_h_all = h_t[u * c.ICN + c.ICN - 1]
                    for d in range(c.NCH):
                        dsg = u * c.NCH + d
                        pd = dsg % 2
                        if moe:
                            img = dwt_c[u * c.NCH + d][:, :]
                            kd = ("dw", u * c.NCH + d)
                        else:
                            img = sdwt_c[d][:, (u - c.E) * c.ICN * 128:
                                            ((u - c.E) * c.ICN + c.ICN) * 128]
                            kd = ("sdw", d * c.NSHU + (u - c.E))
                        tdma = B.dma("sync", f"sDS{pd}", DSL[pd], img,
                                     waits=[("sPE", dsl_stop.get(dsg - 2, 0))]
                                     + img_wait(kd))
                        stops = 0
                        for tgi in range(c.NTG):
                            bank = PS[4 + (grpc % 3)]
                            guard = down_grp[grpc - 3] if grpc >= 3 else 0

                            def d_mms(e, pd=pd, tgi=tgi, bank=bank):
                                last = None
                                for ic in range(c.ICN):
                                    last = e.matmul(
                                        bank[0:128, 0:c.TG],
                                        DSL[pd][:, ic * 128:(ic + 1) * 128],
                                        HBUF[:, ic * c.HALF + tgi * c.TG:
                                             ic * c.HALF + (tgi + 1) * c.TG],
                                        start=(ic == 0), stop=(ic == c.ICN - 1))
                                return last

                            tstop = B.op("tensor", d_mms, inc="sPE",
                                         waits=[(f"sDS{pd}", tdma), ("sDV", tok_h_all),
                                                ("sDV", guard)])
                            xs = xsl(base, d, half * c.HALF + tgi * c.TG, c.TG)
                            tev = B.op("vector", lambda e, xs=xs, bank=bank:
                                       e.scalar_tensor_tensor(
                                           xs, bank[0:128, 0:c.TG], 1.0, xs,
                                           OP.mult, OP.add),
                                       inc="sDV", waits=[("sPE", tstop)])
                            down_grp.append(tev)
                            grpc += 1
                            stops = tstop
                        dsl_stop[dsg] = stops
                        last_down_stop = stops
                        if step_hook is not None:
                            step_hook()
                    if unit_hook is not None:
                        unit_hook(u)
                return cnt["sDV"]

            emit_cast(90)
            if "ffnA" in phases:
                ffn_half(R1, 0, start_waits=[("sLD", tok_setup),
                                            ("sDV", tok_setup_dv)],
                         step_hook=lambda: emit_cast(2))
            emit_cast(len(jobs))
            dump_t = []
            snap_dv = cnt["sDV"]
            for ch in range(c.NCH):
                dump_t.append(B.dma("scalar", "sPK", park[0, ch], xsl(R1, ch, 0, c.HALF),
                                    waits=[("sDV", snap_dv)]))
            snap = [(s, v) for s, v in cnt.items() if v > 0]
            for ch in range(c.NCH):
                B.dma("sync", "sLD", xsl(R1, ch, c.HALF, c.HALF),
                      xin[ch * 128:(ch + 1) * 128, c.HALF:c.T],
                      waits=snap)
            B.fence()
            if "ffnA" in phases:
                ffn_half(R1, 1)
            snap = [(s, v) for s, v in cnt.items() if v > 0]
            for ch in range(c.NCH):
                B.dma("sync", "sLD", xsl(R2, ch, 0, c.HALF), park[0, ch], waits=snap)
            B.fence()

            xbase = {0: R2, 1: R1}
            KFM = cv("KFM", BF16)
            VTM = cv("VTM", BF16)
            QT = cv("QT", BF16)
            YOT = cv("YOT", BF16)
            AWB = [cv("AWB", BF16, off_b=0, nelem=c.NCH * 128),
                   cv("AWB", BF16, off_b=c.NCH * 128 * bf2, nelem=c.NCH * 128)]
            CSLc = cv("CSLc", F32)
            CSLs = cv("CSLs", F32)
            KB = cv("KB", F32)
            KB2 = cv("KB2", F32)
            VTMP = cv("VTMP", BF16)
            PRN2 = [cv("PRN", BF16, off_b=0, nelem=384),
                    cv("PRN", BF16, off_b=768, nelem=384),
                    cv("PRN", BF16, off_b=1536, nelem=384)]
            PRT2 = [cv("PRT", BF16, off_b=0, nelem=384),
                    cv("PRT", BF16, off_b=768, nelem=384)]
            ANM = cv("ANM", F32, nelem=2)
            ADS = cv("ADS", F32, nelem=2)
            ARC = cv("ARC", F32, nelem=2)
            AES = cv("AES", F32, nelem=2)
            MTB = cv("mtab", F32, nelem=8)
            PTB = [psb, ps5[:, 0:192].bitcast(BF16)]
            SCB = [ps2, ps0, ps1]
            AVB = [ps3, ps4]
            XSA = [cv("XSQA", F32, off_b=0, nelem=c.APT),
                   cv("XSQA", F32, off_b=c.APT * f4, nelem=c.APT)]
            ASRA = cv("XSQA", F32, parts=1, off_b=0, nelem=c.APT)
            ASRB = cv("XSQA", F32, parts=1, off_b=c.APT * f4, nelem=c.APT)

            def attn_stats_y(pt, guard_toks):
                t0 = pt * c.APT
                half = t0 // c.HALF
                base = xbase[half]
                mm_t = [0, 0]
                last_mm = 0
                for ch in range(c.NCH):
                    par = ch % 2
                    ta = B.op("scalar", lambda e, ch=ch, par=par: e.activation(
                        XSA[par], xsl(base, ch, t0, c.APT), AF.Square),
                        inc="sAC", waits=[("sPE", mm_t[par])] + guard_toks)
                    last_mm = B.op("tensor", lambda e, ch=ch, par=par: (
                        e.matmul(ps5[0:1, 0:c.APT], onesap[:, 0:1],
                                 xsl(base, ch, t0, c.APT),
                                 start=(ch == 0), stop=(ch == c.NCH - 1)),
                        e.matmul(ps6[0:1, 0:c.APT], onesap[:, 0:1], XSA[par],
                                 start=(ch == 0), stop=(ch == c.NCH - 1)))[-1],
                        inc="sPE", waits=[("sAC", ta)] + guard_toks)
                    mm_t[par] = last_mm
                tv = B.op("vector", lambda e: (
                    e.tensor_scalar_mul(ASRB, ps6[0:1, 0:c.APT], 1.0 / c.DM),
                    e.tensor_scalar_mul(ASRA, ps5[0:1, 0:c.APT], 1.0 / c.DM))[-1],
                    inc="sDV", waits=[("sPE", last_mm)])
                ta = B.op("scalar", lambda e: e.activation(
                    ps5[0:1, 0:c.APT], ASRA, AF.Square),
                    inc="sAC", waits=[("sDV", tv)])
                tv = B.op("vector", lambda e: e.tensor_sub(ASRB, ASRB, ps5[0:1, 0:c.APT]),
                          inc="sDV", waits=[("sAC", ta)])
                tv = B.op("vector", lambda e: e.tensor_scalar_add(ASRB, ASRB, EPS),
                          inc="sDV", waits=[("sDV", tv)])
                ta = B.op("scalar", lambda e: e.activation(ASRB, ASRB, AF.Sqrt),
                          inc="sAC", waits=[("sDV", tv)])
                tv = B.op("vector", lambda e: e.reciprocal(ASRB, ASRB),
                          inc="sDV", waits=[("sAC", ta)])
                tb = B.op("tensor", lambda e: (
                    e.matmul(ps5[0:128, 0:c.APT], onesap[0:1, 0:128], ASRA,
                             start=True, stop=True),
                    e.matmul(ps6[0:128, 0:c.APT], onesap[0:1, 0:128], ASRB,
                             start=True, stop=True))[-1],
                    inc="sPE", waits=[("sDV", tv)])
                tbc = B.op("vector", lambda e: (
                    e.tensor_copy(KB, ps5[0:128, 0:c.APT]),
                    e.tensor_copy(KB2, ps6[0:128, 0:c.APT]))[-1],
                    inc="sDV", waits=[("sPE", tb)])
                ty = 0
                for ch in range(c.NCH):
                    par = ch % 2
                    tv1 = B.op("vector", lambda e, ch=ch, par=par: e.tensor_sub(
                        XSA[par], xsl(base, ch, t0, c.APT), KB),
                        inc="sDV", waits=[("sDV", tbc)])
                    ty = B.op("vector", lambda e, ch=ch, par=par: e.tensor_mul(
                        YOT[:, ch * c.APT:(ch + 1) * c.APT], XSA[par], KB2),
                        inc="sDV", waits=[("sDV", tv1)])
                return ty

            do_attn1 = ("attn" in phases) or ("attn1" in phases)
            do_attn = "attn" in phases
            awb_free = [0, 0]
            tbias_bank = [0, 0]
            trope_prev = 0
            prev_ty = 0
            pp = 0
            for pt in range(c.NPT if do_attn1 else 0):
                t0 = pt * c.APT
                ty = attn_stats_y(pt, [("sDV", prev_ty)])
                prev_ty = ty
                tcs1 = B.dma("sync", "sCS", CSLc, ctabd[:, t0:t0 + c.APT],
                             waits=[("sDV", trope_prev)])
                tcs = B.dma("sync", "sCS", CSLs, stabd[:, t0:t0 + c.APT])
                trope_prev = max(trope_prev, ty)
                for kv in range(2):
                    cache = wk_c if kv == 0 else wv_c
                    for kc in range(c.KVCH):
                        pa = pp % 2
                        bi = pp % 2
                        taw = B.dma("sync", f"sAW{pa}", AWB[pa], cache[kc][:, :],
                                    waits=[("sPE", awb_free[pa])]
                                    + img_wait(("wk" if kv == 0 else "wv", kc)))
                        bankP = PS[bi]
                        pp += 1

                        def kv_mms(e, pa=pa, bankP=bankP):
                            last = None
                            for j in range(c.NCH):
                                last = e.matmul(
                                    bankP[0:128, 0:c.APT],
                                    AWB[pa][:, j * 128:(j + 1) * 128],
                                    YOT[:, j * c.APT:(j + 1) * c.APT],
                                    start=(j == 0), stop=(j == c.NCH - 1))
                            return last

                        tmm = B.op("tensor", kv_mms, inc="sPE",
                                   waits=[(f"sAW{pa}", taw), ("sCS", tcs), ("sDV", ty),
                                          ("sAC", tbias_bank[bi])])
                        awb_free[pa] = tmm
                        if kv == 0:
                            tb_ = B.op("scalar", lambda e, kc=kc, bankP=bankP: e.activation(
                                KB, bankP[0:128, 0:c.APT], AF.Identity,
                                bias=biaskv[:, kc:kc + 1]),
                                inc="sAC", waits=[("sPE", tmm), ("sDV", trope_prev)])
                            tbias_bank[bi] = tb_
                            tsw = B.op("tensor", lambda e: e.matmul(
                                ps2[0:128, 0:c.APT], psw, KB, start=True, stop=True),
                                inc="sPE", waits=[("sAC", tb_), ("sDV", trope_prev)])
                            tr1 = B.op("vector", lambda e: e.tensor_mul(KB2, KB, CSLc),
                                       inc="sDV", waits=[("sAC", tb_), ("sCS", tcs)])
                            tr2 = B.op("vector", lambda e: e.tensor_mul(
                                KB, ps2[0:128, 0:c.APT], CSLs),
                                inc="sDV", waits=[("sPE", tsw), ("sDV", tr1)])
                            tr3 = B.op("vector", lambda e, kc=kc, t0=t0: e.tensor_add(
                                KFM[:, kc * c.T + t0: kc * c.T + t0 + c.APT], KB2, KB),
                                inc="sDV", waits=[("sDV", tr2)])
                            trope_prev = tr3
                        else:
                            tb_ = B.op("scalar", lambda e, kc=kc, bankP=bankP: e.activation(
                                VTMP, bankP[0:128, 0:c.APT], AF.Identity,
                                bias=biaskv[:, c.KVCH + kc: c.KVCH + kc + 1]),
                                inc="sAC", waits=[("sPE", tmm), ("sDV", trope_prev)])
                            tbias_bank[bi] = tb_
                            for g2 in range(2):
                                for s in range(c.SUBT):
                                    g = kc * 2 + g2
                                    tile = pt * c.SUBT + s
                                    ttp = B.op("tensor", lambda e, g2=g2, s=s: e.transpose(
                                        psb[0:128, 0:64],
                                        VTMP[g2 * 64:(g2 + 1) * 64, s * 128:(s + 1) * 128],
                                        idb[g2 * 64:(g2 + 1) * 64, g2 * 64:(g2 + 1) * 64]),
                                        inc="sPE", waits=[("sAC", tb_),
                                                          ("sDV", trope_prev)])
                                    tev = B.op("vector", lambda e, g=g, tile=tile: e.tensor_copy(
                                        VTM[:, tile * (c.HKV * 64) + g * 64:
                                            tile * (c.HKV * 64) + (g + 1) * 64],
                                        psb[0:128, 0:64]),
                                        inc="sDV", waits=[("sPE", ttp)])
                                    trope_prev = tev
            B.fence()

            do_attn2 = do_attn or ("attn2q" in phases) or ("attn2c" in phases)
            do_cores = do_attn or ("attn2c" in phases)
            do_oproj = do_attn
            awb_free = [0, 0]
            for pt in range(c.NPT if do_attn2 else 0):
                t0 = pt * c.APT
                half = t0 // c.HALF
                base = xbase[half]
                ty = attn_stats_y(pt, [])
                tcs = B.dma("sync", "sCS", CSLc, ctabd[:, t0:t0 + c.APT])
                tcs = B.dma("sync", "sCS", CSLs, stabd[:, t0:t0 + c.APT])
                tbias_bank = [0, 0]
                trope_prev = ty
                last_q_mm = 0
                for qc in range(c.QCH):
                    pa = qc % 2
                    bi = qc % 2
                    taw = B.dma("sync", f"sAW{pa}", AWB[pa], wq_c[qc][:, :],
                                waits=[("sPE", awb_free[pa])]
                                + img_wait(("wq", qc)))
                    bankP = PS[bi]

                    def q_mms(e, pa=pa, bankP=bankP):
                        last = None
                        for j in range(c.NCH):
                            last = e.matmul(
                                bankP[0:128, 0:c.APT],
                                AWB[pa][:, j * 128:(j + 1) * 128],
                                YOT[:, j * c.APT:(j + 1) * c.APT],
                                start=(j == 0), stop=(j == c.NCH - 1))
                        return last

                    tmm = B.op("tensor", q_mms, inc="sPE",
                               waits=[(f"sAW{pa}", taw), ("sCS", tcs), ("sDV", ty),
                                      ("sAC", tbias_bank[bi])])
                    awb_free[pa] = tmm
                    last_q_mm = tmm
                    tb_ = B.op("scalar", lambda e, qc=qc, bankP=bankP: e.activation(
                        KB, bankP[0:128, 0:c.APT], AF.Identity, bias=biasq[:, qc:qc + 1]),
                        inc="sAC", waits=[("sPE", tmm), ("sDV", trope_prev)])
                    tbias_bank[bi] = tb_
                    tsw = B.op("tensor", lambda e: e.matmul(
                        ps2[0:128, 0:c.APT], psw, KB, start=True, stop=True),
                        inc="sPE", waits=[("sAC", tb_), ("sDV", trope_prev)])
                    tr1 = B.op("vector", lambda e: e.tensor_mul(KB2, KB, CSLc),
                               inc="sDV", waits=[("sAC", tb_), ("sCS", tcs)])
                    tr2 = B.op("vector", lambda e: e.tensor_mul(
                        KB, ps2[0:128, 0:c.APT], CSLs),
                        inc="sDV", waits=[("sPE", tsw), ("sDV", tr1)])
                    tr3 = B.op("vector", lambda e, qc=qc: e.tensor_add(
                        QT[:, qc * c.APT:(qc + 1) * c.APT], KB2, KB),
                        inc="sDV", waits=[("sDV", tr2)])
                    trope_prev = tr3
                tok_q = trope_prev
                n_items = (c.SUBT * c.H) if do_cores else 0
                it = []
                for sq in range(c.SUBT):
                    tt_idx = pt * c.SUBT + sq
                    kb0 = max(tt_idx - 1, 0)
                    kb1 = min(tt_idx + 1, c.NT128 - 1)
                    nkb = kb1 - kb0 + 1
                    if tt_idx == 0:
                        msc, mec = 0, 3
                    elif tt_idx == c.NT128 - 1:
                        msc, mec = 1, 4
                    else:
                        msc, mec = 1, 2
                    for h in range(c.H):
                        it.append((sq, h, kb0, nkb, msc, mec))
                tsc_t = [0] * n_items
                pn_t = [0] * n_items
                ex_t = [0] * n_items
                tr_t = [0] * n_items
                ev_t = [0] * n_items
                av_t = [0] * n_items
                ot_t = [0] * n_items
                proj_ac = max(tbias_bank)
                for i in range(n_items + 3):
                    if i < n_items:
                        sq, h, kb0, nkb, msc, mec = it[i]
                        W = nkb * 128
                        p = i % 2
                        g = h // c.GRP
                        kc = g // 2
                        gp = 64 * (g % 2)
                        qp = c.h_base[h]
                        qc_ = c.h_chunk[h]
                        p3 = i % 3
                        tsc_t[i] = B.op("tensor", lambda e, sq=sq, kb0=kb0, W=W,
                                        kc=kc, gp=gp, qp=qp, qc_=qc_, p3=p3: e.matmul(
                            SCB[p3][0:128, 0:W],
                            QT[qp:qp + 64, qc_ * c.APT + sq * 128:
                               qc_ * c.APT + (sq + 1) * 128],
                            KFM[gp:gp + 64, kc * c.T + kb0 * 128:
                                kc * c.T + kb0 * 128 + W],
                            start=True, stop=True),
                            inc="sPE",
                            waits=[("sDV", tok_q),
                                   ("sAC", max(ex_t[i - 3] if i >= 3 else 0,
                                               proj_ac))])
                    if i >= 2 and i - 2 < n_items:
                        j = i - 2
                        sqj, hj, kb0j, nkbj, _, _ = it[j]
                        pj = j % 2
                        ttp = 0
                        for kb in range(nkbj):
                            ttp = B.op("tensor", lambda e, kb=kb, pj=pj: e.transpose(
                                PTB[pj][0:128, kb * 128:(kb + 1) * 128],
                                PRT2[pj][:, kb * 128:(kb + 1) * 128], idb),
                                inc="sPE",
                                waits=[("sDV", pn_t[j]),
                                       ("sDV", max(ev_t[j - 2] if j >= 2 else 0, ty))])
                        tr_t[j] = ttp
                    if i >= 3 and i - 3 < n_items:
                        j = i - 3
                        sqj, hj, kb0j, nkbj, _, _ = it[j]
                        pj = j % 2
                        gj = hj // c.GRP

                        def av_mms(e, kb0=kb0j, nkb=nkbj, g=gj, j3=j % 3, pj=pj):
                            last = None
                            for kb in range(nkb):
                                tile = kb0 + kb
                                last = e.matmul(
                                    AVB[pj][0:64, 0:128],
                                    VTM[:, tile * (c.HKV * 64) + g * 64:
                                        tile * (c.HKV * 64) + (g + 1) * 64],
                                    PRN2[j3][:, kb * 128:(kb + 1) * 128],
                                    start=(kb == 0), stop=(kb == nkb - 1))
                            return last

                        av_t[j] = B.op("tensor", av_mms, inc="sPE",
                                       waits=[("sDV", ev_t[j]),
                                              ("sDV", ot_t[j - 2] if j >= 2 else 0)])
                    if i < n_items:
                        sq, h, kb0, nkb, msc, mec = it[i]
                        W = nkb * 128
                        p = i % 2
                        p3 = i % 3
                        moff = 128 if msc == 0 else 0
                        tm0 = B.op("vector", lambda e, p3=p3, W=W, moff=moff:
                                   e.tensor_add(
                                       SCB[p3][0:128, 0:W], SCB[p3][0:128, 0:W],
                                       masks[:, moff:moff + W]),
                                   inc="sDV",
                                   waits=[("sPE", tsc_t[i]),
                                          ("sAC", ex_t[i - 3] if i >= 3 else 0)])
                        tmr = B.op("vector", lambda e, W=W, p=p, p3=p3: e.reduce_max(
                            ANM[:, p:p + 1], SCB[p3][0:128, 0:W], AX.X, negate=True),
                            inc="sDV",
                            waits=[("sDV", tm0),
                                   ("sAC", ex_t[i - 2] if i >= 2 else 0)])
                        tnm = B.op("vector", lambda e, p=p, h=h: e.tensor_scalar_min(
                            ANM[:, p:p + 1], ANM[:, p:p + 1],
                            sinkrep[:, c.H + h: c.H + h + 1]),
                            inc="sDV", waits=[("sDV", tmr)])
                        tex = B.op("scalar", lambda e, W=W, p=p, p3=p3: e.activation(
                            PRN2[p3][:, 0:W], SCB[p3][0:128, 0:W], AF.Exp,
                            bias=ANM[:, p:p + 1], accum_out=ADS[:, p:p + 1]),
                            inc="sAC",
                            waits=[("sDV", tnm),
                                   ("sPE", av_t[i - 3] if i >= 3 else 0)])
                        tes = B.op("scalar", lambda e, p=p, h=h: e.activation(
                            AES[:, p:p + 1], ANM[:, p:p + 1], AF.Exp,
                            bias=sinkrep[:, h:h + 1]),
                            inc="sAC", waits=[("sAC", tex)])
                        ex_t[i] = tes
                        tde = B.op("vector", lambda e, p=p: e.tensor_add(
                            ADS[:, p:p + 1], ADS[:, p:p + 1], AES[:, p:p + 1]),
                            inc="sDV", waits=[("sAC", tes)])
                        trc = B.op("vector", lambda e, p=p: e.reciprocal(
                            ARC[:, p:p + 1], ADS[:, p:p + 1]),
                            inc="sDV", waits=[("sDV", tde)])
                        pn_t[i] = B.op("vector", lambda e, W=W, p=p, p3=p3: e.tensor_scalar_mul(
                            PRT2[p][:, 0:W], PRN2[p3][:, 0:W], ARC[:, p:p + 1]),
                            inc="sDV",
                            waits=[("sDV", trc),
                                   ("sPE", tr_t[i - 2] if i >= 2 else 0)])
                    if i >= 2 and i - 2 < n_items:
                        j = i - 2
                        pj = j % 2
                        Wj = it[j][3] * 128
                        ev_t[j] = B.op("vector", lambda e, Wj=Wj, pj=pj, j3=j % 3:
                                       e.tensor_copy(
                            PRN2[j3][:, 0:Wj], PTB[pj][0:128, 0:Wj]),
                            inc="sDV",
                            waits=[("sPE", tr_t[j]),
                                   ("sPE", av_t[j - 3] if j >= 3 else 0)])
                    if i >= 3 and i - 3 < n_items:
                        j = i - 3
                        sqj, hj = it[j][0], it[j][1]
                        pj = j % 2
                        qpj = c.h_base[hj]
                        qcj = c.h_chunk[hj]
                        ot_t[j] = B.op("vector", lambda e, sqj=sqj, qpj=qpj,
                                       qcj=qcj, pj=pj: e.tensor_copy(
                            YOT[qpj:qpj + 64, qcj * c.APT + sqj * 128:
                                qcj * c.APT + (sqj + 1) * 128],
                            AVB[pj][0:64, 0:128]),
                            inc="sDV", waits=[("sPE", av_t[j]),
                                              ("sPE", last_q_mm)])
                tok_attn = ot_t[n_items - 1] if n_items else 0
                tbo_prev = {0: 0, 1: 0}
                prev_xadd = 0
                for m in range(c.NCH if do_oproj else 0):
                    pa = m % 2
                    taw = B.dma("sync", f"sAW{pa}", AWB[pa], wo_c[m][:, :],
                                waits=[("sPE", awb_free[pa])]
                                + img_wait(("wo", m)))
                    bankO = PS[3 + pa]

                    def o_mms(e, pa=pa, bankO=bankO):
                        last = None
                        for j2 in range(c.QCH):
                            last = e.matmul(
                                bankO[0:128, 0:c.APT],
                                AWB[pa][:, j2 * 128:(j2 + 1) * 128],
                                YOT[:, j2 * c.APT:(j2 + 1) * c.APT],
                                start=(j2 == 0), stop=(j2 == c.QCH - 1))
                        return last

                    tmo = B.op("tensor", o_mms, inc="sPE",
                               waits=[(f"sAW{pa}", taw), ("sDV", tok_attn),
                                      ("sAC", tbo_prev[pa])])
                    awb_free[pa] = tmo
                    tbo = B.op("scalar", lambda e, m=m, bankO=bankO: e.activation(
                        KB, bankO[0:128, 0:c.APT], AF.Identity, bias=biaso[:, m:m + 1]),
                        inc="sAC", waits=[("sPE", tmo), ("sDV", prev_xadd)])
                    tbo_prev[pa] = tbo
                    xs = xsl(base, m, t0, c.APT)
                    prev_xadd = B.op("vector", lambda e, xs=xs: e.tensor_add(xs, xs, KB),
                                     inc="sDV", waits=[("sAC", tbo)])
                B.fence()

            snap_dv = cnt["sDV"]
            for ch in range(c.NCH):
                B.dma("scalar", "sPK", park[0, ch], xsl(R2, ch, 0, c.HALF),
                      waits=[("sDV", snap_dv)])
            B.fence()
            if "ffnB" in phases:
                ffn_half(R1, 1)
            snap_dv = cnt["sDV"]
            dump1 = []
            for ch in range(c.NCH):
                dump1.append(B.dma("scalar", "sPK", park[1, ch], xsl(R1, ch, 0, c.HALF),
                                   waits=[("sDV", snap_dv)]))
            snap = [(s, v) for s, v in cnt.items() if v > 0]
            for ch in range(c.NCH):
                B.dma("sync", "sLD", xsl(R1, ch, 0, c.HALF), park[0, ch], waits=snap)
            B.fence()
            if "ffnB" in phases:
                ffn_half(R1, 0)
            B.fence()

            snap_dv = cnt["sDV"]
            for ch in range(c.NCH):
                B.dma("scalar", "sPK", xout[ch * 128:(ch + 1) * 128, 0:c.HALF],
                      xsl(R1, ch, 0, c.HALF), waits=[("sDV", snap_dv)])
            for ch in range(c.NCH):
                B.dma("gpsimd", "sGL0", xout[ch * 128:(ch + 1) * 128, c.HALF:c.T],
                      park[1, ch], waits=[("sPK", max(dump1))])
            B.wait("scalar", [("sPK", cnt["sPK"])])
            B.wait("gpsimd", [("sGL0", cnt["sGL0"])])
            B.wait("sync", [("sPK", cnt["sPK"]), ("sGL0", cnt["sGL0"])])

        final_cnts = []
        for name in ENGINES:
            def run(eng, name=name):
                B.begin(name, eng)
                schedule()
                final_cnts.append(dict(B.cnt))
            getattr(block, name)(run)
        assert all(fc == final_cnts[0] for fc in final_cnts), (
            "schedule replay diverged across engines")

    return nc



def prep_shared(cfg, inputs):
    c = cfg

    def t(x):
        return np.ascontiguousarray(np.asarray(x, dtype=np.float32))

    gw = t(inputs["L0_ffn_gate_w"]).reshape(c.E * c.ICN, 128, c.NCH, 128)
    uw = t(inputs["L0_ffn_up_w"]).reshape(c.E * c.ICN, 128, c.NCH, 128)
    dw = t(inputs["L0_ffn_down_w"]).reshape(c.E * c.NCH, 128, c.ICN, 128)
    sg = t(inputs["L0_ffn_shared_gate_w"]).reshape(c.ISCN, 128, c.NCH, 128)
    su = t(inputs["L0_ffn_shared_up_w"]).reshape(c.ISCN, 128, c.NCH, 128)
    sd = t(inputs["L0_ffn_shared_down_w"]).reshape(c.NCH, 128, c.ISCN, 128)
    wq = t(inputs["L0_attn_Wq"]).reshape(c.DM, c.H * 64)[:, c.q_perm]
    wk = t(inputs["L0_attn_Wk"]).reshape(c.DM, c.HKV * 64)
    wv = t(inputs["L0_attn_Wv"]).reshape(c.DM, c.HKV * 64)
    wo = t(inputs["L0_attn_Wo"]).reshape(c.H * 64, c.DM)[c.q_perm, :]

    def oc_tile(w, noc):
        return np.ascontiguousarray(
            w.reshape(c.NCH, 128, noc, 128).transpose(2, 1, 0, 3)
            .reshape(noc, 128, c.NCH * 128))

    m = dict(
        gwf=np.ascontiguousarray(gw.transpose(0, 3, 2, 1).reshape(
            c.E * c.ICN, 128, c.NCH * 128)),
        uwf=np.ascontiguousarray(uw.transpose(0, 3, 2, 1).reshape(
            c.E * c.ICN, 128, c.NCH * 128)),
        dwf=np.ascontiguousarray(dw.transpose(0, 3, 2, 1).reshape(
            c.E * c.NCH, 128, c.ICN * 128)),
        sgw=np.ascontiguousarray(sg.transpose(0, 3, 2, 1).reshape(
            c.ISCN, 128, c.NCH * 128)),
        suw=np.ascontiguousarray(su.transpose(0, 3, 2, 1).reshape(
            c.ISCN, 128, c.NCH * 128)),
        sdw=np.ascontiguousarray(sd.transpose(0, 3, 2, 1).reshape(
            c.NCH, 128, c.ISCN * 128)),
        wq=oc_tile(wq, c.QCH),
        wk=oc_tile(wk, c.KVCH),
        wv=oc_tile(wv, c.KVCH),
        wo=np.ascontiguousarray(
            wo.reshape(c.QCH, 128, c.NCH, 128).transpose(2, 1, 0, 3)
            .reshape(c.NCH, 128, c.QCH * 128)),
        rw=np.ascontiguousarray(np.asarray(inputs["L0_ffn_router_w"], np.float32)),
        sew=np.ascontiguousarray(np.asarray(inputs["L0_ffn_se_gate_w"], np.float32).reshape(1, c.DM)),
        snk=np.ascontiguousarray(np.asarray(inputs["L0_attn_sinks"], np.float32).reshape(1, c.H)),
        bq=np.ascontiguousarray(np.asarray(inputs["L0_attn_bq"], np.float32).reshape(c.H * 64)[c.q_perm].reshape(c.QCH, 128).T),
        bkv=np.ascontiguousarray(np.concatenate([
            np.asarray(inputs["L0_attn_bk"], np.float32).reshape(c.KVCH, 128).T,
            np.asarray(inputs["L0_attn_bv"], np.float32).reshape(c.KVCH, 128).T], axis=1)),
        bo=np.ascontiguousarray(np.asarray(inputs["L0_attn_bo"], np.float32).reshape(c.NCH, 128).T),
    )
    return m


def prep_inputs(cfg, inputs, core):
    m = dict(prep_shared(cfg, inputs))
    x = np.asarray(inputs["L0_attn_hidden_states"][core], dtype=np.float32)
    m["xin"] = np.ascontiguousarray(x.T)
    return m


_PROG_CACHE = {}


def kernel(**inputs):
    cfg = CFG_FULL
    consts = make_consts(cfg)
    nc = build_program(cfg)
    ncores = 8
    shared = prep_shared(cfg, inputs)
    shared.update({k: np.ascontiguousarray(v) for k, v in consts.items()})
    in_maps = []
    for core in range(ncores):
        m = dict(shared)
        x = np.asarray(inputs["L0_attn_hidden_states"][core], dtype=np.float32)
        m["xin"] = np.ascontiguousarray(x.T)
        in_maps.append(m)
    res = run_bass_kernel_spmd(nc, in_maps, core_ids=list(range(ncores)))
    out = np.empty((ncores, cfg.T, cfg.DM), dtype=np.float32)
    for core in range(ncores):
        out[core] = res.results[core]["xout"].reshape(cfg.DM, cfg.T).T
    return out


# version 12
# speedup vs baseline: 1.2040x; 1.2040x over previous
import numpy as np
import ml_dtypes
import concourse.bass as bass
import concourse.mybir as mybir
from concourse.bass_utils import run_bass_kernel_spmd

F32 = mybir.dt.float32
BF16 = mybir.dt.bfloat16
U8 = mybir.dt.uint8
AF = mybir.ActivationFunctionType
AX = mybir.AxisListType
OP = mybir.AluOpType

NEG = -1.0e9
EPS = 1e-5
WAVELEN = 150000.0
ENGINES = ("sync", "scalar", "vector", "tensor", "gpsimd")


class CFG:
    def __init__(self, DM=2048, T=2048, H=32, HKV=8, E=16, I=1408, IS=5632,
                 TG=512, APT=256):
        self.DM, self.T, self.H, self.HKV, self.E, self.I, self.IS = DM, T, H, HKV, E, I, IS
        self.D = 64
        self.NCH = DM // 128
        self.GRP = H // HKV
        self.QCH = H * 64 // 128
        self.KVCH = HKV * 64 // 128
        self.ICN = I // 128
        self.ISCN = IS // 128
        self.NSHU = self.ISCN // self.ICN
        assert self.ISCN % self.ICN == 0
        self.HALF = T // 2
        self.TG = TG
        self.NTG = self.HALF // TG
        assert self.NTG == 2
        self.APT = APT
        self.NPT = T // APT
        self.SUBT = APT // 128
        self.NT128 = T // 128
        self.NSEL = E + 1
        self.NU = E + self.NSHU
        self.SCALE = self.D ** (-0.25)
        self.h_chunk = [((h // self.GRP) // 2) * self.GRP + (h % self.GRP)
                        for h in range(H)]
        self.h_base = [64 * ((h // self.GRP) % 2) for h in range(H)]
        perm = np.zeros(H * 64, dtype=np.int64)
        for h in range(H):
            dst = self.h_chunk[h] * 128 + self.h_base[h]
            perm[dst:dst + 64] = np.arange(h * 64, (h + 1) * 64)
        self.q_perm = perm


CFG_FULL = CFG()
CFG_MINI = CFG(DM=256, T=512, H=4, HKV=2, E=8, I=256, IS=512, TG=128, APT=128)


def make_consts(cfg):
    bf = ml_dtypes.bfloat16
    identb = np.eye(128, dtype=np.float32).astype(bf)
    identf = np.eye(128, dtype=np.float32)
    pswap = np.zeros((128, 128), dtype=np.float32)
    for i in range(128):
        pswap[i, i ^ 1] = 1.0
    ones = np.ones((128, 128), dtype=np.float32)
    a = np.arange(128)[:, None]
    b = np.arange(128)[None, :]
    masks = np.zeros((128, 384), dtype=np.float32)
    masks[:, 0:128] = np.where(a <= b, 0.0, NEG)
    masks[:, 256:384] = np.where(b <= a, 0.0, NEG)
    nsel = cfg.NSEL
    sel = np.zeros((nsel, nsel * 128), dtype=np.float32)
    for e in range(nsel):
        sel[e, e * 128:(e + 1) * 128] = 1.0
    sel = sel.astype(bf)
    half = cfg.D // 2
    inv_freq = (1.0 / WAVELEN ** (np.arange(half, dtype=np.float32) * 2.0 / cfg.D)).astype(np.float32)
    t = np.arange(cfg.T, dtype=np.float32)
    freqs = t[:, None] * inv_freq[None, :]
    cosv = np.cos(freqs).astype(np.float32)
    sinv = np.sin(freqs).astype(np.float32)
    p = np.arange(128)
    fidx = (p % 64) // 2
    sign = np.where(p % 2 == 0, -1.0, 1.0).astype(np.float32)
    ctab = np.ascontiguousarray((cosv[:, fidx].T * cfg.SCALE).astype(np.float32))
    stab = np.ascontiguousarray((sinv[:, fidx].T * cfg.SCALE * sign[:, None]).astype(np.float32))
    iota = np.arange(128, dtype=np.float32)
    mtab = np.zeros((128, 8), dtype=np.float32)
    mtab[:, 1] = iota
    mtab[:, 2] = iota + 257.0
    mtab[:, 3] = iota + 129.0
    mtab[:, 4] = 256.0
    return dict(identb=identb, identf=identf, pswap=pswap, ones=ones,
                masks=masks, sel=sel, ctab=ctab, stab=stab, mtab=mtab)


class Bld:

    def __init__(self, sems):
        self.sems = sems
        self.cur = None
        self.eng = None
        self.cnt = None

    def begin(self, name, eng):
        self.cur = name
        self.eng = eng
        self.cnt = {k: 0 for k in self.sems}

    def _waits(self, waits):
        w = {}
        for s, v in waits:
            if v:
                w[s] = max(w.get(s, 0), v)
        for s, v in w.items():
            self.eng.wait_ge(self.sems[s], v)

    def dma(self, q, sem, out, in_, waits=(), **kw):
        self.cnt[sem] += 16
        if self.cur == q:
            self._waits(waits)
            self.eng.dma_start(out=out, in_=in_, **kw).then_inc(self.sems[sem], 16)
        return self.cnt[sem]

    def dmat(self, q, sem, out, in_, waits=()):
        self.cnt[sem] += 16
        if self.cur == q:
            self._waits(waits)
            self.eng.dma_start_transpose(out, in_).then_inc(self.sems[sem], 16)
        return self.cnt[sem]

    def op(self, engine, fn, inc=None, waits=()):
        tok = None
        if inc is not None:
            self.cnt[inc] += 1
            tok = self.cnt[inc]
        if self.cur == engine:
            self._waits(waits)
            inst = fn(self.eng)
            if inc is not None:
                assert inst is not None
                inst.then_inc(self.sems[inc], 1)
        return tok

    def wait(self, engine, waits):
        if self.cur == engine:
            self._waits(waits)

    def fence(self, engines=ENGINES):
        snap = [(s, v) for s, v in self.cnt.items() if v > 0]
        for e in engines:
            self.wait(e, snap)


def build_program(cfg, phases=("cast", "ffnA", "attn", "ffnB")):
    nc = bass.Bass("TRN2", target_bir_lowering=False)
    phases = set(phases)
    c = cfg
    bf2, f4 = 2, 4

    def din(name, shape, dt=F32):
        return nc.dram_tensor(name, shape, dt, kind="ExternalInput").ap()

    xin = din("xin", [c.DM, c.T])
    gwf = din("gwf", [c.E * c.ICN, 128, c.NCH * 128])
    uwf = din("uwf", [c.E * c.ICN, 128, c.NCH * 128])
    dwf = din("dwf", [c.E * c.NCH, 128, c.ICN * 128])
    sgw = din("sgw", [c.ISCN, 128, c.NCH * 128])
    suw = din("suw", [c.ISCN, 128, c.NCH * 128])
    sdw = din("sdw", [c.NCH, 128, c.ISCN * 128])
    rw = din("rw", [c.E, c.DM])
    sew = din("sew", [1, c.DM])
    snk = din("snk", [1, c.H])
    wqd = din("wq", [c.QCH, 128, c.NCH * 128])
    wkd = din("wk", [c.KVCH, 128, c.NCH * 128])
    wvd = din("wv", [c.KVCH, 128, c.NCH * 128])
    wod = din("wo", [c.NCH, 128, c.QCH * 128])
    bqd = din("bq", [128, c.QCH])
    bkvd = din("bkv", [128, 2 * c.KVCH])
    bod = din("bo", [128, c.NCH])
    identbd = din("identb", [128, 128], BF16)
    identfd = din("identf", [128, 128])
    pswapd = din("pswap", [128, 128])
    onesd = din("ones", [128, 128])
    masksd = din("masks", [128, 384])
    seld = din("sel", [c.NSEL, c.NSEL * 128], BF16)
    ctabd = din("ctab", [128, c.T])
    stabd = din("stab", [128, c.T])
    mtabd = din("mtab", [128, 8])

    xout = nc.dram_tensor("xout", [c.DM, c.T], F32, kind="ExternalOutput").ap()

    gwt_c = nc.dram_tensor("gwt_c", [c.E * c.ICN, 128, c.DM], BF16, kind="Internal").ap()
    uwt_c = nc.dram_tensor("uwt_c", [c.E * c.ICN, 128, c.DM], BF16, kind="Internal").ap()
    sgwt_c = nc.dram_tensor("sgwt_c", [c.ISCN, 128, c.DM], BF16, kind="Internal").ap()
    suwt_c = nc.dram_tensor("suwt_c", [c.ISCN, 128, c.DM], BF16, kind="Internal").ap()
    dwt_c = nc.dram_tensor("dwt_c", [c.E * c.NCH, 128, c.I], BF16, kind="Internal").ap()
    sdwt_c = nc.dram_tensor("sdwt_c", [c.NCH, 128, c.IS], BF16, kind="Internal").ap()
    wq_c = nc.dram_tensor("wq_c", [c.QCH, 128, c.NCH * 128], BF16, kind="Internal").ap()
    wk_c = nc.dram_tensor("wk_c", [c.KVCH, 128, c.NCH * 128], BF16, kind="Internal").ap()
    wv_c = nc.dram_tensor("wv_c", [c.KVCH, 128, c.NCH * 128], BF16, kind="Internal").ap()
    wo_c = nc.dram_tensor("wo_c", [c.NCH, 128, c.QCH * 128], BF16, kind="Internal").ap()
    park = nc.dram_tensor("park", [2, c.NCH, 128, c.HALF], F32, kind="Internal").ap()

    RB = c.NCH * c.HALF * f4
    R1, R2, P0 = 0, RB, 2 * RB

    def mklay(items, base):
        lay, off = {}, base
        for name, sz in items:
            sz = (sz + 63) & ~63
            lay[name] = (off, sz)
            off += sz
        return lay, off

    ffn_r2_items = [
        ("YH", c.NCH * c.HALF * bf2),
        ("HBUF", c.ICN * c.HALF * bf2),
        ("HS", c.HALF * bf2),
        ("UB", c.HALF * bf2),
    ]
    ffn_r2, ffn2_end = mklay(ffn_r2_items, R2)

    ffn_pool, ffnp_end = mklay([
        ("GSL", 2 * c.DM * bf2),
        ("USL", 2 * c.DM * bf2),
        ("DSL", 2 * c.ICN * 128 * bf2),
        ("WBC", 2 * c.HALF * f4),
        ("MBC", c.TG * f4), ("RBC", c.TG * f4),
        ("YS", 2 * c.TG * f4),
        ("YSF", 2 * c.TG * f4),
        ("LGT", c.TG * f4),
        ("LROW", (c.TG // 128) * 32 * f4),
        ("WST", (c.TG // 128) * (c.NSEL + 3) * bf2),
        ("T8", 8 * f4), ("NMr", 64), ("DSr", 64), ("RCr", 64),
        ("PEX", c.E * f4), ("MSK", c.E * f4),
    ], P0)
    if ffn2_end > P0:
        ffn_r2, ffnp_end = mklay(ffn_r2_items, ffnp_end)

    castw = c.NCH * 128 * f4
    cast_pool, cast_end = mklay([
        ("CINF", 2 * castw),
        ("CINB", 2 * (castw // 2)),
    ], ffnp_end)
    setup_items = [
        ("RWL", c.DM * f4),
        ("SEW", c.DM * f4),
        ("SNKL", 64 * f4),
    ]
    setup_pool, setup_end = mklay(setup_items, R2)
    if setup_end > P0:
        setup_pool, setup_end = mklay(setup_items, cast_end)

    attn_pool, attn_end = mklay([
        ("KFM", c.KVCH * c.T * bf2),
        ("VTM", c.NT128 * c.HKV * 64 * bf2),
        ("QT", c.QCH * c.APT * bf2),
        ("YOT", c.NCH * c.APT * bf2),
        ("AWB", 2 * c.NCH * 128 * bf2),
        ("CSLc", c.APT * f4), ("CSLs", c.APT * f4),
        ("KB", c.APT * f4), ("KB2", c.APT * f4),
        ("VTMP", c.APT * bf2),
        ("PRN", 3 * 384 * bf2), ("PRT", 2 * 384 * bf2),
        ("ANM", 64), ("ADS", 64), ("ARC", 64), ("AES", 64),
        ("XSQA", 2 * c.APT * f4),
    ], P0)

    pool_end = max(ffnp_end, cast_end, attn_end, setup_end)
    assert cast_end <= P0 + 71000, cast_end
    const_lay, const_end = mklay([
        ("identb", 128 * bf2), ("identf", 128 * f4), ("pswap", 128 * f4),
        ("ones", 128 * f4), ("masks", 384 * f4), ("sel", c.NSEL * 128 * bf2),
        ("biasq", c.QCH * f4), ("biaskv", 2 * c.KVCH * f4), ("biaso", c.NCH * f4),
        ("sinkrep", 2 * c.H * f4), ("rwseT", c.NCH * c.NSEL * f4),
        ("wTs", c.HALF * bf2), ("mtab", 8 * f4),
    ], pool_end)
    ARENA = const_end
    assert ARENA <= 212800, f"SBUF over budget: {ARENA}"

    lay = {}
    for d in (ffn_r2, ffn_pool, cast_pool, setup_pool, attn_pool, const_lay):
        lay.update(d)

    SEM_NAMES = ["sLD", "sPK",
                 "sGL0", "sGL1", "sTR0", "sTR1", "sST0", "sST1",
                 "sGU0", "sGU1", "sDS0", "sDS1", "sAW0", "sAW1", "sCS",
                 "sPE", "sDV", "sAC"]
    from contextlib import ExitStack
    with ExitStack() as stack:
        arena = stack.enter_context(nc.sbuf_tensor("arena", [128, ARENA], U8))
        ps0 = stack.enter_context(nc.psum_tensor("ps0", [128, 512], F32))
        ps1 = stack.enter_context(nc.psum_tensor("ps1", [128, 512], F32))
        ps2 = stack.enter_context(nc.psum_tensor("ps2", [128, 512], F32))
        ps3 = stack.enter_context(nc.psum_tensor("ps3", [128, 512], F32))
        ps4 = stack.enter_context(nc.psum_tensor("ps4", [128, 512], F32))
        ps5 = stack.enter_context(nc.psum_tensor("ps5", [128, 512], F32))
        ps6 = stack.enter_context(nc.psum_tensor("ps6", [128, 512], F32))
        psb = stack.enter_context(nc.psum_tensor("psb", [128, 768], BF16))
        sems = {nm: stack.enter_context(nc.semaphore(nm)) for nm in SEM_NAMES}
        block = stack.enter_context(nc.Block())
        PS = [ps0, ps1, ps2, ps3, ps4, ps5, ps6]
        B = Bld(sems)

        def cv(name, dtype, parts=128, off_b=0, nelem=None):
            off, sz = lay[name]
            dsz = {F32: 4, BF16: 2, U8: 1}[dtype]
            nb = sz - off_b if nelem is None else nelem * dsz
            return arena[0:parts, off + off_b: off + off_b + nb].bitcast(dtype)

        def xsl(base, ch, t0g, n):
            off = base + (ch * c.HALF + (t0g % c.HALF)) * f4
            return arena[:, off: off + n * f4].bitcast(F32)

        def schedule():
            cnt = B.cnt
            idb = cv("identb", BF16)
            idf = cv("identf", F32)
            psw = cv("pswap", F32)
            onesap = cv("ones", F32)
            masks = cv("masks", F32)
            selap = cv("sel", BF16, parts=c.NSEL)
            biasq = cv("biasq", F32)
            biaskv = cv("biaskv", F32)
            biaso = cv("biaso", F32)
            sinkrep = cv("sinkrep", F32)
            rwseT = cv("rwseT", F32)
            wTs = cv("wTs", BF16, parts=c.NSEL)

            tcl = []
            for nm, dt_, src in (("identb", BF16, identbd), ("identf", F32, identfd),
                                 ("pswap", F32, pswapd), ("ones", F32, onesd),
                                 ("masks", F32, masksd)):
                tcl.append(B.dma("sync", "sLD", cv(nm, dt_), src[:, :]))
            tcl.append(B.dma("sync", "sLD", selap, seld[:, :]))
            tcl.append(B.dma("sync", "sLD", cv("biasq", F32, nelem=c.QCH), bqd[:, :]))
            tcl.append(B.dma("sync", "sLD", cv("biaskv", F32, nelem=2 * c.KVCH), bkvd[:, :]))
            tcl.append(B.dma("sync", "sLD", cv("biaso", F32, nelem=c.NCH), bod[:, :]))
            tcl.append(B.dma("sync", "sLD", cv("mtab", F32, nelem=8), mtabd[:, :]))
            for ch in range(c.NCH):
                tcl.append(B.dma("sync", "sLD", xsl(R1, ch, 0, c.HALF),
                                 xin[ch * 128:(ch + 1) * 128, 0:c.HALF]))
            t_rw = B.dma("sync", "sLD", cv("RWL", F32, parts=c.E), rw[:, :])
            t_se = B.dma("sync", "sLD", cv("SEW", F32, parts=1), sew[:, :])
            t_sk = B.dma("sync", "sLD", cv("SNKL", F32, parts=1, nelem=c.H), snk[:, :])
            tok_setup = cnt["sLD"]
            rwl = cv("RWL", F32, parts=c.E)
            sewr = cv("SEW", F32, parts=1)
            snkl = cv("SNKL", F32, parts=1, nelem=c.H)

            prev_ev = 0
            for j in range(c.NCH):
                tmm = B.op("tensor", lambda e, j=j: (
                    e.transpose(ps5[0:128, 0:c.E], rwl[0:c.E, j * 128:(j + 1) * 128],
                                idf[0:c.E, 0:c.E]),
                    e.transpose(ps6[0:128, 0:1], sewr[0:1, j * 128:(j + 1) * 128],
                                idf[0:1, 0:1]))[-1],
                    inc="sPE",
                    waits=[("sLD", tok_setup), ("sDV", prev_ev)])
                prev_ev = B.op("vector", lambda e, j=j: (
                    e.tensor_copy(rwseT[:, j * c.NSEL: j * c.NSEL + c.E], ps5[0:128, 0:c.E]),
                    e.tensor_copy(rwseT[:, j * c.NSEL + c.E: (j + 1) * c.NSEL],
                                  ps6[0:128, 0:1]))[-1],
                    inc="sDV", waits=[("sPE", tmm)])
            tmm = B.op("tensor", lambda e: e.matmul(
                ps5[0:128, 0:c.H], onesap[0:1, 0:128], snkl[0:1, 0:c.H],
                start=True, stop=True),
                inc="sPE", waits=[("sLD", tok_setup), ("sDV", prev_ev)])
            B.op("vector", lambda e: (
                e.tensor_copy(sinkrep[:, 0:c.H], ps5[0:128, 0:c.H]),
                e.tensor_scalar_mul(sinkrep[:, c.H:2 * c.H], ps5[0:128, 0:c.H], -1.0))[-1],
                inc="sDV", waits=[("sPE", tmm)])
            tok_setup_dv = cnt["sDV"]

            CW = c.NCH * 128
            CINF = [cv("CINF", F32, off_b=0, nelem=CW),
                    cv("CINF", F32, off_b=castw, nelem=CW)]
            CINB = [cv("CINB", BF16, off_b=0, nelem=CW),
                    cv("CINB", BF16, off_b=castw // 2, nelem=CW)]
            jobs = []
            for e_ in range(c.E):
                for ic in range(c.ICN):
                    sidx = e_ * c.ICN + ic
                    jobs.append((gwf[sidx], CW, gwt_c[sidx][:, :], ("gw", sidx)))
                    jobs.append((uwf[sidx], CW, uwt_c[sidx][:, :], ("uw", sidx)))
                for d_ in range(c.NCH):
                    sidx = e_ * c.NCH + d_
                    jobs.append((dwf[sidx], c.ICN * 128, dwt_c[sidx][:, :],
                                 ("dw", sidx)))
            for j_ in range(c.NSHU):
                for ic in range(c.ICN):
                    sidx = j_ * c.ICN + ic
                    jobs.append((sgw[sidx], CW, sgwt_c[sidx][:, :], ("sgw", sidx)))
                    jobs.append((suw[sidx], CW, suwt_c[sidx][:, :], ("suw", sidx)))
            IW = c.ICN * 128
            for q_ in range(c.NSHU):
                for d_ in range(c.NCH):
                    jobs.append((sdw[d_][:, q_ * IW:(q_ + 1) * IW], IW,
                                 sdwt_c[d_][:, q_ * IW:(q_ + 1) * IW],
                                 ("sdw", d_ * c.NSHU + q_)))
            for j_ in range(c.QCH):
                jobs.append((wqd[j_], CW, wq_c[j_][:, :], ("wq", j_)))
            for j_ in range(c.KVCH):
                jobs.append((wkd[j_], CW, wk_c[j_][:, :], ("wk", j_)))
                jobs.append((wvd[j_], CW, wv_c[j_][:, :], ("wv", j_)))
            for j_ in range(c.NCH):
                jobs.append((wod[j_], c.QCH * 128, wo_c[j_][:, :], ("wo", j_)))

            conv_tok = [0, 0]
            store_tok = [0, 0]
            stok = {}
            if "cast" not in phases:
                jobs = []
            cast_cursor = [0]

            def emit_cast(n):
                lo = cast_cursor[0]
                hi = min(lo + n, len(jobs))
                cast_cursor[0] = hi
                for si in range(lo, hi):
                    src, w, img, key = jobs[si]
                    p = si % 2
                    tl = B.dma("sync", f"sTR{p}", CINF[p][:, 0:w], src,
                               waits=[("sAC", conv_tok[p])])
                    tc = B.op("scalar", lambda e, p=p, w=w: e.activation(
                        CINB[p][:, 0:w], CINF[p][:, 0:w], AF.Identity),
                        inc="sAC",
                        waits=[(f"sTR{p}", tl), (f"sST{p}", store_tok[p])])
                    conv_tok[p] = tc
                    ts = B.dma("scalar", f"sST{p}", img, CINB[p][:, 0:w],
                               waits=[("sAC", tc)])
                    store_tok[p] = ts
                    stok[key] = (f"sST{p}", ts)

            def img_wait(key):
                if key in stok:
                    return [stok[key]]
                return []

            YH = cv("YH", BF16)
            HBUF = cv("HBUF", BF16)
            HS = cv("HS", BF16)
            UB = cv("UB", BF16)
            GSL = [cv("GSL", BF16, off_b=0, nelem=c.DM),
                   cv("GSL", BF16, off_b=c.DM * bf2, nelem=c.DM)]
            USL = [cv("USL", BF16, off_b=0, nelem=c.DM),
                   cv("USL", BF16, off_b=c.DM * bf2, nelem=c.DM)]
            DSL = [cv("DSL", BF16, off_b=0, nelem=c.ICN * 128),
                   cv("DSL", BF16, off_b=c.ICN * 128 * bf2, nelem=c.ICN * 128)]
            WBC = [cv("WBC", F32, off_b=0, nelem=c.HALF),
                   cv("WBC", F32, off_b=c.HALF * f4, nelem=c.HALF)]
            MBC = cv("MBC", F32)
            RBC = cv("RBC", F32)
            XSQ = [cv("YS", F32, off_b=0, nelem=c.TG),
                   cv("YS", F32, off_b=c.TG * f4, nelem=c.TG)]
            SRA = cv("YS", F32, parts=1, off_b=0, nelem=c.TG)
            SRB = cv("YS", F32, parts=1, off_b=c.TG * f4, nelem=c.TG)
            YS = [cv("YS", F32, off_b=0, nelem=c.TG),
                  cv("YS", F32, off_b=c.TG * f4, nelem=c.TG)]
            YSF = [cv("YSF", F32, off_b=0, nelem=c.TG),
                   cv("YSF", F32, off_b=c.TG * f4, nelem=c.TG)]
            LGT = cv("LGT", F32, parts=c.NSEL)
            LROW = cv("LROW", F32)
            WSTb = cv("WST", BF16)
            T8 = cv("T8", F32, nelem=8)
            NMr = cv("NMr", F32, nelem=1)
            DSr = cv("DSr", F32, nelem=1)
            RCr = cv("RCr", F32, nelem=1)
            PEXr = cv("PEX", F32, nelem=c.E)
            MSKr = cv("MSK", F32, nelem=c.E)

            def ln_router(base, half, with_router, start_waits=()):
                wts_toks = []
                yread = [0, 0]
                for tg in range(c.NTG):
                    t0 = half * c.HALF + tg * c.TG
                    sq_t = [0, 0]
                    mm_t = [0, 0]
                    last_mm = 0
                    for ch in range(c.NCH):
                        par = ch % 2
                        ta = B.op("scalar", lambda e, ch=ch, par=par: e.activation(
                            XSQ[par], xsl(base, ch, t0, c.TG), AF.Square),
                            inc="sAC", waits=[("sPE", mm_t[par]),
                                              ("sDV", yread[par])] + list(start_waits))
                        last_mm = B.op("tensor", lambda e, ch=ch, par=par: (
                            e.matmul(ps5[0:1, 0:c.TG], onesap[:, 0:1],
                                     xsl(base, ch, t0, c.TG),
                                     start=(ch == 0), stop=(ch == c.NCH - 1)),
                            e.matmul(ps6[0:1, 0:c.TG], onesap[:, 0:1], XSQ[par],
                                     start=(ch == 0), stop=(ch == c.NCH - 1)))[-1],
                            inc="sPE", waits=[("sAC", ta)] + list(start_waits))
                        mm_t[par] = last_mm
                    tv = B.op("vector", lambda e: (
                        e.tensor_scalar_mul(SRB, ps6[0:1, 0:c.TG], 1.0 / c.DM),
                        e.tensor_scalar_mul(SRA, ps5[0:1, 0:c.TG], 1.0 / c.DM))[-1],
                        inc="sDV", waits=[("sPE", last_mm), ("sDV", max(yread))])
                    ta = B.op("scalar", lambda e: e.activation(
                        ps5[0:1, 0:c.TG], SRA, AF.Square),
                        inc="sAC", waits=[("sDV", tv)])
                    tv = B.op("vector", lambda e: e.tensor_sub(
                        SRB, SRB, ps5[0:1, 0:c.TG]),
                        inc="sDV", waits=[("sAC", ta)])
                    tv = B.op("vector", lambda e: e.tensor_scalar_add(SRB, SRB, EPS),
                              inc="sDV", waits=[("sDV", tv)])
                    ta = B.op("scalar", lambda e: e.activation(SRB, SRB, AF.Sqrt),
                              inc="sAC", waits=[("sDV", tv)])
                    tv = B.op("vector", lambda e: e.reciprocal(SRB, SRB),
                              inc="sDV", waits=[("sAC", ta)])
                    tb = B.op("tensor", lambda e: (
                        e.matmul(ps5[0:128, 0:c.TG], onesap[0:1, 0:128], SRA,
                                 start=True, stop=True),
                        e.matmul(ps6[0:128, 0:c.TG], onesap[0:1, 0:128], SRB,
                                 start=True, stop=True))[-1],
                        inc="sPE", waits=[("sDV", tv)])
                    tbc = B.op("vector", lambda e: (
                        e.tensor_copy(MBC, ps5[0:128, 0:c.TG]),
                        e.tensor_copy(RBC, ps6[0:128, 0:c.TG]))[-1],
                        inc="sDV", waits=[("sPE", tb)])
                    rt = [0, 0]
                    last_r = 0
                    tv_tok = [0, 0]
                    for ch in range(c.NCH):
                        par = ch % 2
                        tv1 = B.op("vector", lambda e, ch=ch, par=par: e.tensor_sub(
                            YS[par], xsl(base, ch, t0, c.TG), MBC),
                            inc="sDV", waits=[("sDV", tbc), ("sPE", rt[par])])
                        tv2 = B.op("vector", lambda e, par=par: e.tensor_mul(
                            YSF[par], YS[par], RBC),
                            inc="sDV", waits=[("sDV", tv1)])
                        yread[par] = tv2
                        tv3 = B.op("vector", lambda e, ch=ch, par=par: e.tensor_copy(
                            YH[:, ch * c.HALF + tg * c.TG: ch * c.HALF + (tg + 1) * c.TG],
                            YSF[par]),
                            inc="sDV", waits=[("sDV", tv2)])
                        tv_tok[par] = tv3
                        if with_router:
                            last_r = B.op("tensor", lambda e, ch=ch, par=par: e.matmul(
                                ps4[0:c.NSEL, 0:c.TG],
                                rwseT[:, ch * c.NSEL:(ch + 1) * c.NSEL], YSF[par],
                                start=(ch == 0), stop=(ch == c.NCH - 1)),
                                inc="sPE", waits=[("sDV", tv2)])
                            rt[par] = last_r
                    if not with_router:
                        continue
                    te = B.op("vector", lambda e: e.tensor_copy(
                        LGT[0:c.NSEL, 0:c.TG], ps4[0:c.NSEL, 0:c.TG]),
                        inc="sDV", waits=[("sPE", last_r)])
                    prev_lrow = 0
                    for s in range(c.TG // 128):
                        tt = B.op("tensor", lambda e, s=s: e.transpose(
                            ps5[0:128, 0:c.NSEL], LGT[0:c.NSEL, s * 128:(s + 1) * 128],
                            idf[0:c.NSEL, 0:c.NSEL]),
                            inc="sPE", waits=[("sDV", te), ("sDV", prev_lrow)])
                        tl = B.op("vector", lambda e, s=s: e.tensor_copy(
                            LROW[:, s * 32: s * 32 + c.NSEL], ps5[0:128, 0:c.NSEL]),
                            inc="sDV", waits=[("sPE", tt)])
                        prev_lrow = tl
                        t8t = B.op("vector", lambda e, s=s: e.max(
                            out=T8, in_=LROW[:, s * 32: s * 32 + c.E]),
                            inc="sDV", waits=[("sDV", tl)])
                        tnm = B.op("vector", lambda e: e.tensor_scalar_mul(
                            NMr, T8[:, 0:1], -1.0),
                            inc="sDV", waits=[("sDV", t8t)])
                        tex = B.op("scalar", lambda e, s=s: e.activation(
                            PEXr, LROW[:, s * 32: s * 32 + c.E], AF.Exp,
                            bias=NMr, accum_out=DSr),
                            inc="sAC", waits=[("sDV", tnm)])
                        trc = B.op("vector", lambda e: e.reciprocal(RCr, DSr),
                                   inc="sDV", waits=[("sAC", tex)])
                        tmk = B.op("vector", lambda e, s=s: e.tensor_single_scalar(
                            MSKr, LROW[:, s * 32: s * 32 + c.E], T8[:, 3:4], OP.is_ge),
                            inc="sDV", waits=[("sDV", t8t)])
                        tw1 = B.op("vector", lambda e: e.tensor_mul(PEXr, PEXr, MSKr),
                                   inc="sDV", waits=[("sDV", tmk), ("sDV", trc)])
                        tw2 = B.op("vector", lambda e, s=s: e.tensor_scalar(
                            WSTb[:, s * (c.NSEL + 3): s * (c.NSEL + 3) + c.E],
                            PEXr, RCr, 0.5, OP.mult, OP.mult),
                            inc="sDV", waits=[("sDV", tw1)])
                        tsg = B.op("scalar", lambda e, s=s: e.activation(
                            DSr, LROW[:, s * 32 + c.E: s * 32 + c.E + 1], AF.Sigmoid),
                            inc="sAC", waits=[("sDV", trc)])
                        tsg2 = B.op("vector", lambda e, s=s: e.tensor_scalar_mul(
                            WSTb[:, s * (c.NSEL + 3) + c.E: s * (c.NSEL + 3) + c.NSEL],
                            DSr, 0.5),
                            inc="sDV", waits=[("sAC", tsg)])
                        ttb = B.op("tensor", lambda e, s=s: e.transpose(
                            psb[0:c.NSEL, s * 128:(s + 1) * 128],
                            WSTb[:, s * (c.NSEL + 3): s * (c.NSEL + 3) + c.NSEL], idb),
                            inc="sPE", waits=[("sDV", tw2), ("sDV", tsg2)])
                        twt = B.op("vector", lambda e, s=s, tg=tg: e.tensor_copy(
                            wTs[0:c.NSEL, tg * c.TG + s * 128: tg * c.TG + (s + 1) * 128],
                            psb[0:c.NSEL, s * 128:(s + 1) * 128]),
                            inc="sDV", waits=[("sPE", ttb)])
                        wts_toks.append(twt)
                return wts_toks

            def ffn_half(base, half, start_waits=(), unit_hook=None,
                         step_hook=None):
                wts_toks = ln_router(base, half, True, start_waits)
                tok_wts = max(wts_toks)
                guA = {}
                guU = {}
                silu_t = {}
                ub_t = {}
                h_t = {}
                dsl_stop = {}
                down_grp = []
                grpc = 0
                last_down_stop = 0
                last_wbc_mm = 0
                for u in range(c.NU):
                    moe = u < c.E
                    erow = u if moe else c.E
                    up = u % 2
                    wbc_ev = []
                    for tgi in range(c.NTG):
                        bank = PS[4 + (grpc % 3)]
                        guard = down_grp[grpc - 3] if grpc >= 3 else 0
                        tw = B.op("tensor", lambda e, erow=erow, tgi=tgi, bank=bank: e.matmul(
                            bank[0:128, 0:c.TG],
                            selap[0:c.NSEL, erow * 128:(erow + 1) * 128],
                            wTs[0:c.NSEL, tgi * c.TG:(tgi + 1) * c.TG],
                            start=True, stop=True),
                            inc="sPE", waits=[("sDV", guard), ("sDV", tok_wts)])
                        last_wbc_mm = tw
                        tv = B.op("vector", lambda e, tgi=tgi, up=up, bank=bank: e.tensor_copy(
                            WBC[up][:, tgi * c.TG:(tgi + 1) * c.TG], bank[0:128, 0:c.TG]),
                            inc="sDV", waits=[("sPE", tw)])
                        down_grp.append(tv)
                        wbc_ev.append(tv)
                        grpc += 1
                    tok_wbc = max(wbc_ev)
                    for ic in range(c.ICN):
                        gsi = u * c.ICN + ic
                        p = gsi % 2
                        if moe:
                            img_g = gwt_c[u * c.ICN + ic]
                            img_u = uwt_c[u * c.ICN + ic]
                            kg = ("gw", u * c.ICN + ic)
                            ku = ("uw", u * c.ICN + ic)
                        else:
                            img_g = sgwt_c[(u - c.E) * c.ICN + ic]
                            img_u = suwt_c[(u - c.E) * c.ICN + ic]
                            kg = ("sgw", (u - c.E) * c.ICN + ic)
                            ku = ("suw", (u - c.E) * c.ICN + ic)
                        free_tok = guU.get(gsi - 2, 0)
                        B.dma("sync", f"sGU{p}", GSL[p], img_g[:, :],
                              waits=[("sPE", free_tok)] + img_wait(kg))
                        wl = B.dma("sync", f"sGU{p}", USL[p], img_u[:, :],
                                   waits=img_wait(ku))

                        def g_mms(e, p=p, half=half):
                            last = None
                            for j in range(c.NCH):
                                lhs = GSL[p][:, j * 128:(j + 1) * 128]
                                for tgi in range(c.NTG):
                                    last = e.matmul(
                                        PS[tgi][0:128, 0:c.TG], lhs,
                                        YH[:, j * c.HALF + tgi * c.TG: j * c.HALF + (tgi + 1) * c.TG],
                                        start=(j == 0), stop=(j == c.NCH - 1))
                            return last

                        def u_mms(e, p=p, half=half):
                            last = None
                            for j in range(c.NCH):
                                lhs = USL[p][:, j * 128:(j + 1) * 128]
                                for tgi in range(c.NTG):
                                    last = e.matmul(
                                        PS[2 + tgi][0:128, 0:c.TG], lhs,
                                        YH[:, j * c.HALF + tgi * c.TG: j * c.HALF + (tgi + 1) * c.TG],
                                        start=(j == 0), stop=(j == c.NCH - 1))
                            return last

                        gA = B.op("tensor", g_mms, inc="sPE",
                                  waits=[(f"sGU{p}", wl), ("sAC", silu_t.get(gsi - 1, 0))])
                        uA = B.op("tensor", u_mms, inc="sPE",
                                  waits=[("sDV", ub_t.get(gsi - 1, 0))])
                        guA[gsi], guU[gsi] = gA, uA
                        ts_ = B.op("scalar", lambda e: (
                            e.activation(HS[:, 0:c.TG], ps0[0:128, 0:c.TG], AF.Silu),
                            e.activation(HS[:, c.TG:2 * c.TG], ps1[0:128, 0:c.TG], AF.Silu))[-1],
                            inc="sAC",
                            waits=[("sPE", gA), ("sDV", h_t.get(gsi - 1, 0))])
                        tu_ = B.op("vector", lambda e, up=up: (
                            e.tensor_mul(UB[:, 0:c.TG], ps2[0:128, 0:c.TG], WBC[up][:, 0:c.TG]),
                            e.tensor_mul(UB[:, c.TG:2 * c.TG], ps3[0:128, 0:c.TG],
                                         WBC[up][:, c.TG:2 * c.TG]))[-1],
                            inc="sDV",
                            waits=[("sPE", uA), ("sDV", max(h_t.get(gsi - 1, 0), tok_wbc)),
                                   ("sPE", last_down_stop)])
                        th_ = B.op("vector", lambda e, ic=ic: e.tensor_mul(
                            HBUF[:, ic * c.HALF:(ic + 1) * c.HALF], HS[:, 0:c.HALF],
                            UB[:, 0:c.HALF]),
                            inc="sDV",
                            waits=[("sAC", ts_), ("sDV", tu_), ("sPE", last_down_stop)])
                        silu_t[gsi], ub_t[gsi], h_t[gsi] = ts_, tu_, th_
                        if step_hook is not None:
                            step_hook()
                    tok_h_all = h_t[u * c.ICN + c.ICN - 1]
                    for d in range(c.NCH):
                        dsg = u * c.NCH + d
                        pd = dsg % 2
                        if moe:
                            img = dwt_c[u * c.NCH + d][:, :]
                            kd = ("dw", u * c.NCH + d)
                        else:
                            img = sdwt_c[d][:, (u - c.E) * c.ICN * 128:
                                            ((u - c.E) * c.ICN + c.ICN) * 128]
                            kd = ("sdw", d * c.NSHU + (u - c.E))
                        tdma = B.dma("sync", f"sDS{pd}", DSL[pd], img,
                                     waits=[("sPE", dsl_stop.get(dsg - 2, 0))]
                                     + img_wait(kd))
                        stops = 0
                        for tgi in range(c.NTG):
                            bank = PS[4 + (grpc % 3)]
                            guard = down_grp[grpc - 3] if grpc >= 3 else 0

                            def d_mms(e, pd=pd, tgi=tgi, bank=bank):
                                last = None
                                for ic in range(c.ICN):
                                    last = e.matmul(
                                        bank[0:128, 0:c.TG],
                                        DSL[pd][:, ic * 128:(ic + 1) * 128],
                                        HBUF[:, ic * c.HALF + tgi * c.TG:
                                             ic * c.HALF + (tgi + 1) * c.TG],
                                        start=(ic == 0), stop=(ic == c.ICN - 1))
                                return last

                            tstop = B.op("tensor", d_mms, inc="sPE",
                                         waits=[(f"sDS{pd}", tdma), ("sDV", tok_h_all),
                                                ("sDV", guard)])
                            xs = xsl(base, d, half * c.HALF + tgi * c.TG, c.TG)
                            tev = B.op("vector", lambda e, xs=xs, bank=bank:
                                       e.scalar_tensor_tensor(
                                           xs, bank[0:128, 0:c.TG], 1.0, xs,
                                           OP.mult, OP.add),
                                       inc="sDV", waits=[("sPE", tstop)])
                            down_grp.append(tev)
                            grpc += 1
                            stops = tstop
                        dsl_stop[dsg] = stops
                        last_down_stop = stops
                        if step_hook is not None:
                            step_hook()
                    if unit_hook is not None:
                        unit_hook(u)
                return cnt["sDV"]

            emit_cast(90)
            if "ffnA" in phases:
                ffn_half(R1, 0, start_waits=[("sLD", tok_setup),
                                            ("sDV", tok_setup_dv)],
                         step_hook=lambda: emit_cast(2))
            emit_cast(len(jobs))
            dump_t = []
            snap_dv = cnt["sDV"]
            for ch in range(c.NCH):
                dump_t.append(B.dma("scalar", "sPK", park[0, ch], xsl(R1, ch, 0, c.HALF),
                                    waits=[("sDV", snap_dv)]))
            snap = [(s, v) for s, v in cnt.items() if v > 0]
            for ch in range(c.NCH):
                B.dma("sync", "sLD", xsl(R1, ch, c.HALF, c.HALF),
                      xin[ch * 128:(ch + 1) * 128, c.HALF:c.T],
                      waits=snap)
            B.fence()
            if "ffnA" in phases:
                ffn_half(R1, 1)
            snap = [(s, v) for s, v in cnt.items() if v > 0]
            for ch in range(c.NCH):
                B.dma("sync", "sLD", xsl(R2, ch, 0, c.HALF), park[0, ch], waits=snap)
            B.fence()

            xbase = {0: R2, 1: R1}
            KFM = cv("KFM", BF16)
            VTM = cv("VTM", BF16)
            QT = cv("QT", BF16)
            YOT = cv("YOT", BF16)
            AWB = [cv("AWB", BF16, off_b=0, nelem=c.NCH * 128),
                   cv("AWB", BF16, off_b=c.NCH * 128 * bf2, nelem=c.NCH * 128)]
            CSLc = cv("CSLc", F32)
            CSLs = cv("CSLs", F32)
            KB = cv("KB", F32)
            KB2 = cv("KB2", F32)
            VTMP = cv("VTMP", BF16)
            PRN2 = [cv("PRN", BF16, off_b=0, nelem=384),
                    cv("PRN", BF16, off_b=768, nelem=384),
                    cv("PRN", BF16, off_b=1536, nelem=384)]
            PRT2 = [cv("PRT", BF16, off_b=0, nelem=384),
                    cv("PRT", BF16, off_b=768, nelem=384)]
            ANM = cv("ANM", F32, nelem=2)
            ADS = cv("ADS", F32, nelem=2)
            ARC = cv("ARC", F32, nelem=2)
            AES = cv("AES", F32, nelem=2)
            MTB = cv("mtab", F32, nelem=8)
            PTB = [psb, ps5[:, 0:192].bitcast(BF16)]
            SCB = [ps2, ps0, ps1]
            AVB = [ps3, ps4]
            XSA = [cv("XSQA", F32, off_b=0, nelem=c.APT),
                   cv("XSQA", F32, off_b=c.APT * f4, nelem=c.APT)]
            ASRA = cv("XSQA", F32, parts=1, off_b=0, nelem=c.APT)
            ASRB = cv("XSQA", F32, parts=1, off_b=c.APT * f4, nelem=c.APT)

            def attn_stats_y(pt, guard_toks):
                t0 = pt * c.APT
                half = t0 // c.HALF
                base = xbase[half]
                mm_t = [0, 0]
                last_mm = 0
                for ch in range(c.NCH):
                    par = ch % 2
                    ta = B.op("scalar", lambda e, ch=ch, par=par: e.activation(
                        XSA[par], xsl(base, ch, t0, c.APT), AF.Square),
                        inc="sAC", waits=[("sPE", mm_t[par])] + guard_toks)
                    last_mm = B.op("tensor", lambda e, ch=ch, par=par: (
                        e.matmul(ps5[0:1, 0:c.APT], onesap[:, 0:1],
                                 xsl(base, ch, t0, c.APT),
                                 start=(ch == 0), stop=(ch == c.NCH - 1)),
                        e.matmul(ps6[0:1, 0:c.APT], onesap[:, 0:1], XSA[par],
                                 start=(ch == 0), stop=(ch == c.NCH - 1)))[-1],
                        inc="sPE", waits=[("sAC", ta)] + guard_toks)
                    mm_t[par] = last_mm
                tv = B.op("vector", lambda e: (
                    e.tensor_scalar_mul(ASRB, ps6[0:1, 0:c.APT], 1.0 / c.DM),
                    e.tensor_scalar_mul(ASRA, ps5[0:1, 0:c.APT], 1.0 / c.DM))[-1],
                    inc="sDV", waits=[("sPE", last_mm)])
                ta = B.op("scalar", lambda e: e.activation(
                    ps5[0:1, 0:c.APT], ASRA, AF.Square),
                    inc="sAC", waits=[("sDV", tv)])
                tv = B.op("vector", lambda e: e.tensor_sub(ASRB, ASRB, ps5[0:1, 0:c.APT]),
                          inc="sDV", waits=[("sAC", ta)])
                tv = B.op("vector", lambda e: e.tensor_scalar_add(ASRB, ASRB, EPS),
                          inc="sDV", waits=[("sDV", tv)])
                ta = B.op("scalar", lambda e: e.activation(ASRB, ASRB, AF.Sqrt),
                          inc="sAC", waits=[("sDV", tv)])
                tv = B.op("vector", lambda e: e.reciprocal(ASRB, ASRB),
                          inc="sDV", waits=[("sAC", ta)])
                tb = B.op("tensor", lambda e: (
                    e.matmul(ps5[0:128, 0:c.APT], onesap[0:1, 0:128], ASRA,
                             start=True, stop=True),
                    e.matmul(ps6[0:128, 0:c.APT], onesap[0:1, 0:128], ASRB,
                             start=True, stop=True))[-1],
                    inc="sPE", waits=[("sDV", tv)])
                tbc = B.op("vector", lambda e: (
                    e.tensor_copy(KB, ps5[0:128, 0:c.APT]),
                    e.tensor_copy(KB2, ps6[0:128, 0:c.APT]))[-1],
                    inc="sDV", waits=[("sPE", tb)])
                ty = 0
                for ch in range(c.NCH):
                    par = ch % 2
                    tv1 = B.op("vector", lambda e, ch=ch, par=par: e.tensor_sub(
                        XSA[par], xsl(base, ch, t0, c.APT), KB),
                        inc="sDV", waits=[("sDV", tbc)])
                    ty = B.op("vector", lambda e, ch=ch, par=par: e.tensor_mul(
                        YOT[:, ch * c.APT:(ch + 1) * c.APT], XSA[par], KB2),
                        inc="sDV", waits=[("sDV", tv1)])
                return ty

            do_attn1 = ("attn" in phases) or ("attn1" in phases)
            do_attn = "attn" in phases
            awb_free = [0, 0]
            tbias_bank = [0, 0]
            trope_prev = 0
            prev_ty = 0
            pp = 0
            for pt in range(c.NPT if do_attn1 else 0):
                t0 = pt * c.APT
                ty = attn_stats_y(pt, [("sDV", prev_ty)])
                prev_ty = ty
                tcs1 = B.dma("sync", "sCS", CSLc, ctabd[:, t0:t0 + c.APT],
                             waits=[("sDV", trope_prev)])
                tcs = B.dma("sync", "sCS", CSLs, stabd[:, t0:t0 + c.APT])
                trope_prev = max(trope_prev, ty)
                for kv in range(2):
                    cache = wk_c if kv == 0 else wv_c
                    for kc in range(c.KVCH):
                        pa = pp % 2
                        bi = pp % 2
                        taw = B.dma("sync", f"sAW{pa}", AWB[pa], cache[kc][:, :],
                                    waits=[("sPE", awb_free[pa])]
                                    + img_wait(("wk" if kv == 0 else "wv", kc)))
                        bankP = PS[bi]
                        pp += 1

                        def kv_mms(e, pa=pa, bankP=bankP):
                            last = None
                            for j in range(c.NCH):
                                last = e.matmul(
                                    bankP[0:128, 0:c.APT],
                                    AWB[pa][:, j * 128:(j + 1) * 128],
                                    YOT[:, j * c.APT:(j + 1) * c.APT],
                                    start=(j == 0), stop=(j == c.NCH - 1))
                            return last

                        tmm = B.op("tensor", kv_mms, inc="sPE",
                                   waits=[(f"sAW{pa}", taw), ("sCS", tcs), ("sDV", ty),
                                          ("sAC", tbias_bank[bi])])
                        awb_free[pa] = tmm
                        if kv == 0:
                            tb_ = B.op("scalar", lambda e, kc=kc, bankP=bankP: e.activation(
                                KB, bankP[0:128, 0:c.APT], AF.Identity,
                                bias=biaskv[:, kc:kc + 1]),
                                inc="sAC", waits=[("sPE", tmm), ("sDV", trope_prev)])
                            tbias_bank[bi] = tb_
                            tsw = B.op("tensor", lambda e: e.matmul(
                                ps2[0:128, 0:c.APT], psw, KB, start=True, stop=True),
                                inc="sPE", waits=[("sAC", tb_), ("sDV", trope_prev)])
                            tr1 = B.op("vector", lambda e: e.tensor_mul(KB2, KB, CSLc),
                                       inc="sDV", waits=[("sAC", tb_), ("sCS", tcs)])
                            tr2 = B.op("vector", lambda e: e.tensor_mul(
                                KB, ps2[0:128, 0:c.APT], CSLs),
                                inc="sDV", waits=[("sPE", tsw), ("sDV", tr1)])
                            tr3 = B.op("vector", lambda e, kc=kc, t0=t0: e.tensor_add(
                                KFM[:, kc * c.T + t0: kc * c.T + t0 + c.APT], KB2, KB),
                                inc="sDV", waits=[("sDV", tr2)])
                            trope_prev = tr3
                        else:
                            tb_ = B.op("scalar", lambda e, kc=kc, bankP=bankP: e.activation(
                                VTMP, bankP[0:128, 0:c.APT], AF.Identity,
                                bias=biaskv[:, c.KVCH + kc: c.KVCH + kc + 1]),
                                inc="sAC", waits=[("sPE", tmm), ("sDV", trope_prev)])
                            tbias_bank[bi] = tb_
                            for g2 in range(2):
                                for s in range(c.SUBT):
                                    g = kc * 2 + g2
                                    tile = pt * c.SUBT + s
                                    ttp = B.op("tensor", lambda e, g2=g2, s=s: e.transpose(
                                        psb[0:128, 0:64],
                                        VTMP[g2 * 64:(g2 + 1) * 64, s * 128:(s + 1) * 128],
                                        idb[g2 * 64:(g2 + 1) * 64, g2 * 64:(g2 + 1) * 64]),
                                        inc="sPE", waits=[("sAC", tb_),
                                                          ("sDV", trope_prev)])
                                    tev = B.op("vector", lambda e, g=g, tile=tile: e.tensor_copy(
                                        VTM[:, tile * (c.HKV * 64) + g * 64:
                                            tile * (c.HKV * 64) + (g + 1) * 64],
                                        psb[0:128, 0:64]),
                                        inc="sDV", waits=[("sPE", ttp)])
                                    trope_prev = tev
            B.fence()

            do_attn2 = do_attn or ("attn2q" in phases) or ("attn2c" in phases)
            do_cores = do_attn or ("attn2c" in phases)
            do_oproj = do_attn
            awb_free = [0, 0]
            for pt in range(c.NPT if do_attn2 else 0):
                t0 = pt * c.APT
                half = t0 // c.HALF
                base = xbase[half]
                ty = attn_stats_y(pt, [])
                tcs = B.dma("sync", "sCS", CSLc, ctabd[:, t0:t0 + c.APT])
                tcs = B.dma("sync", "sCS", CSLs, stabd[:, t0:t0 + c.APT])
                tbias_bank = [0, 0]
                trope_prev = ty
                last_q_mm = 0
                for qc in range(c.QCH):
                    pa = qc % 2
                    bi = qc % 2
                    taw = B.dma("sync", f"sAW{pa}", AWB[pa], wq_c[qc][:, :],
                                waits=[("sPE", awb_free[pa])]
                                + img_wait(("wq", qc)))
                    bankP = PS[bi]

                    def q_mms(e, pa=pa, bankP=bankP):
                        last = None
                        for j in range(c.NCH):
                            last = e.matmul(
                                bankP[0:128, 0:c.APT],
                                AWB[pa][:, j * 128:(j + 1) * 128],
                                YOT[:, j * c.APT:(j + 1) * c.APT],
                                start=(j == 0), stop=(j == c.NCH - 1))
                        return last

                    tmm = B.op("tensor", q_mms, inc="sPE",
                               waits=[(f"sAW{pa}", taw), ("sCS", tcs), ("sDV", ty),
                                      ("sAC", tbias_bank[bi])])
                    awb_free[pa] = tmm
                    last_q_mm = tmm
                    tb_ = B.op("scalar", lambda e, qc=qc, bankP=bankP: e.activation(
                        KB, bankP[0:128, 0:c.APT], AF.Identity, bias=biasq[:, qc:qc + 1]),
                        inc="sAC", waits=[("sPE", tmm), ("sDV", trope_prev)])
                    tbias_bank[bi] = tb_
                    tsw = B.op("tensor", lambda e: e.matmul(
                        ps2[0:128, 0:c.APT], psw, KB, start=True, stop=True),
                        inc="sPE", waits=[("sAC", tb_), ("sDV", trope_prev)])
                    tr1 = B.op("vector", lambda e: e.tensor_mul(KB2, KB, CSLc),
                               inc="sDV", waits=[("sAC", tb_), ("sCS", tcs)])
                    tr2 = B.op("vector", lambda e: e.tensor_mul(
                        KB, ps2[0:128, 0:c.APT], CSLs),
                        inc="sDV", waits=[("sPE", tsw), ("sDV", tr1)])
                    tr3 = B.op("vector", lambda e, qc=qc: e.tensor_add(
                        QT[:, qc * c.APT:(qc + 1) * c.APT], KB2, KB),
                        inc="sDV", waits=[("sDV", tr2)])
                    trope_prev = tr3
                tok_q = trope_prev
                n_items = (c.SUBT * c.H) if do_cores else 0
                it = []
                for sq in range(c.SUBT):
                    tt_idx = pt * c.SUBT + sq
                    kb0 = max(tt_idx - 1, 0)
                    kb1 = min(tt_idx + 1, c.NT128 - 1)
                    nkb = kb1 - kb0 + 1
                    if tt_idx == 0:
                        msc, mec = 0, 3
                    elif tt_idx == c.NT128 - 1:
                        msc, mec = 1, 4
                    else:
                        msc, mec = 1, 2
                    for h in range(c.H):
                        it.append((sq, h, kb0, nkb, msc, mec))
                tsc_t = [0] * n_items
                tnm_t = [0] * n_items
                add_t = [0] * n_items
                pn_t = [0] * n_items
                ex_t = [0] * n_items
                tr_t = [0] * n_items
                ev_t = [0] * n_items
                av_t = [0] * n_items
                ot_t = [0] * n_items
                proj_ac = max(tbias_bank)
                for i in range(n_items + 3):
                    if i < n_items:
                        sq, h, kb0, nkb, msc, mec = it[i]
                        W = nkb * 128
                        p = i % 2
                        g = h // c.GRP
                        kc = g // 2
                        gp = 64 * (g % 2)
                        qp = c.h_base[h]
                        qc_ = c.h_chunk[h]
                        p3 = i % 3
                        tsc_t[i] = B.op("tensor", lambda e, sq=sq, kb0=kb0, W=W,
                                        kc=kc, gp=gp, qp=qp, qc_=qc_, p3=p3: e.matmul(
                            SCB[p3][0:128, 0:W],
                            QT[qp:qp + 64, qc_ * c.APT + sq * 128:
                               qc_ * c.APT + (sq + 1) * 128],
                            KFM[gp:gp + 64, kc * c.T + kb0 * 128:
                                kc * c.T + kb0 * 128 + W],
                            start=True, stop=True),
                            inc="sPE",
                            waits=[("sDV", tok_q),
                                   ("sAC", max(ex_t[i - 3] if i >= 3 else 0,
                                               proj_ac))])
                    if i >= 2 and i - 2 < n_items:
                        j = i - 2
                        sqj, hj, kb0j, nkbj, _, _ = it[j]
                        pj = j % 2
                        ttp = 0
                        for kb in range(nkbj):
                            ttp = B.op("tensor", lambda e, kb=kb, pj=pj: e.transpose(
                                PTB[pj][0:128, kb * 128:(kb + 1) * 128],
                                PRT2[pj][:, kb * 128:(kb + 1) * 128], idb),
                                inc="sPE",
                                waits=[("sDV", pn_t[j]),
                                       ("sDV", max(ev_t[j - 2] if j >= 2 else 0, ty))])
                        tr_t[j] = ttp
                    if i >= 3 and i - 3 < n_items:
                        j = i - 3
                        sqj, hj, kb0j, nkbj, _, _ = it[j]
                        pj = j % 2
                        gj = hj // c.GRP

                        def av_mms(e, kb0=kb0j, nkb=nkbj, g=gj, j3=j % 3, pj=pj):
                            last = None
                            for kb in range(nkb):
                                tile = kb0 + kb
                                last = e.matmul(
                                    AVB[pj][0:64, 0:128],
                                    VTM[:, tile * (c.HKV * 64) + g * 64:
                                        tile * (c.HKV * 64) + (g + 1) * 64],
                                    PRN2[j3][:, kb * 128:(kb + 1) * 128],
                                    start=(kb == 0), stop=(kb == nkb - 1))
                            return last

                        av_t[j] = B.op("tensor", av_mms, inc="sPE",
                                       waits=[("sDV", ev_t[j]),
                                              ("sDV", ot_t[j - 2] if j >= 2 else 0)])
                    if i < n_items:
                        sq, h, kb0, nkb, msc, mec = it[i]
                        W = nkb * 128
                        p = i % 2
                        p3 = i % 3
                        moff = 128 if msc == 0 else 0
                        tm0 = B.op("vector", lambda e, p3=p3, W=W, moff=moff:
                                   e.tensor_add(
                                       SCB[p3][0:128, 0:W], SCB[p3][0:128, 0:W],
                                       masks[:, moff:moff + W]),
                                   inc="sDV",
                                   waits=[("sPE", tsc_t[i]),
                                          ("sAC", ex_t[i - 3] if i >= 3 else 0)])
                        tmr = B.op("vector", lambda e, W=W, p=p, p3=p3: e.reduce_max(
                            ANM[:, p:p + 1], SCB[p3][0:128, 0:W], AX.X, negate=True),
                            inc="sDV",
                            waits=[("sDV", tm0),
                                   ("sAC", ex_t[i - 2] if i >= 2 else 0)])
                        tnm_t[i] = B.op("vector", lambda e, p=p, h=h: e.tensor_scalar_min(
                            ANM[:, p:p + 1], ANM[:, p:p + 1],
                            sinkrep[:, c.H + h: c.H + h + 1]),
                            inc="sDV", waits=[("sDV", tmr)])
                        tex = B.op("scalar", lambda e, W=W, p=p, p3=p3: e.activation(
                            PRN2[p3][:, 0:W], SCB[p3][0:128, 0:W], AF.Exp,
                            bias=ANM[:, p:p + 1], accum_out=ADS[:, p:p + 1]),
                            inc="sAC",
                            waits=[("sDV", tnm_t[i]),
                                   ("sPE", av_t[i - 3] if i >= 3 else 0),
                                   ("sDV", pn_t[i - 2] if i >= 2 else 0)])
                        tes = B.op("scalar", lambda e, p=p, h=h: e.activation(
                            AES[:, p:p + 1], ANM[:, p:p + 1], AF.Exp,
                            bias=sinkrep[:, h:h + 1]),
                            inc="sAC",
                            waits=[("sAC", tex),
                                   ("sAC", add_t[i - 2] if i >= 2 else 0)])
                        add_t[i] = B.op("scalar", lambda e, p=p: e.activation(
                            ADS[:, p:p + 1], AES[:, p:p + 1], AF.Identity,
                            bias=ADS[:, p:p + 1]),
                            inc="sAC", waits=[("sAC", tes)])
                        ex_t[i] = add_t[i]
                    if i >= 1 and i - 1 < n_items:
                        j = i - 1
                        pj = j % 2
                        Wj = it[j][3] * 128
                        trc = B.op("vector", lambda e, pj=pj: e.reciprocal(
                            ARC[:, pj:pj + 1], ADS[:, pj:pj + 1]),
                            inc="sDV",
                            waits=[("sAC", add_t[j]),
                                   ("sDV", pn_t[j - 2] if j >= 2 else 0)])
                        pn_t[j] = B.op("vector", lambda e, Wj=Wj, pj=pj, j3=j % 3:
                                       e.tensor_scalar_mul(
                            PRT2[pj][:, 0:Wj], PRN2[j3][:, 0:Wj], ARC[:, pj:pj + 1]),
                            inc="sDV",
                            waits=[("sDV", trc),
                                   ("sPE", tr_t[j - 2] if j >= 2 else 0)])
                    if i >= 2 and i - 2 < n_items:
                        j = i - 2
                        pj = j % 2
                        Wj = it[j][3] * 128
                        ev_t[j] = B.op("vector", lambda e, Wj=Wj, pj=pj, j3=j % 3:
                                       e.tensor_copy(
                            PRN2[j3][:, 0:Wj], PTB[pj][0:128, 0:Wj]),
                            inc="sDV",
                            waits=[("sPE", tr_t[j]),
                                   ("sPE", av_t[j - 3] if j >= 3 else 0)])
                    if i >= 3 and i - 3 < n_items:
                        j = i - 3
                        sqj, hj = it[j][0], it[j][1]
                        pj = j % 2
                        qpj = c.h_base[hj]
                        qcj = c.h_chunk[hj]
                        ot_t[j] = B.op("vector", lambda e, sqj=sqj, qpj=qpj,
                                       qcj=qcj, pj=pj: e.tensor_copy(
                            YOT[qpj:qpj + 64, qcj * c.APT + sqj * 128:
                                qcj * c.APT + (sqj + 1) * 128],
                            AVB[pj][0:64, 0:128]),
                            inc="sDV", waits=[("sPE", av_t[j]),
                                              ("sPE", last_q_mm)])
                tok_attn = ot_t[n_items - 1] if n_items else 0
                tbo_prev = {0: 0, 1: 0}
                prev_xadd = 0
                for m in range(c.NCH if do_oproj else 0):
                    pa = m % 2
                    taw = B.dma("sync", f"sAW{pa}", AWB[pa], wo_c[m][:, :],
                                waits=[("sPE", awb_free[pa])]
                                + img_wait(("wo", m)))
                    bankO = PS[3 + pa]

                    def o_mms(e, pa=pa, bankO=bankO):
                        last = None
                        for j2 in range(c.QCH):
                            last = e.matmul(
                                bankO[0:128, 0:c.APT],
                                AWB[pa][:, j2 * 128:(j2 + 1) * 128],
                                YOT[:, j2 * c.APT:(j2 + 1) * c.APT],
                                start=(j2 == 0), stop=(j2 == c.QCH - 1))
                        return last

                    tmo = B.op("tensor", o_mms, inc="sPE",
                               waits=[(f"sAW{pa}", taw), ("sDV", tok_attn),
                                      ("sAC", tbo_prev[pa])])
                    awb_free[pa] = tmo
                    tbo = B.op("scalar", lambda e, m=m, bankO=bankO: e.activation(
                        KB, bankO[0:128, 0:c.APT], AF.Identity, bias=biaso[:, m:m + 1]),
                        inc="sAC", waits=[("sPE", tmo), ("sDV", prev_xadd)])
                    tbo_prev[pa] = tbo
                    xs = xsl(base, m, t0, c.APT)
                    prev_xadd = B.op("vector", lambda e, xs=xs: e.tensor_add(xs, xs, KB),
                                     inc="sDV", waits=[("sAC", tbo)])
                B.fence()

            snap_dv = cnt["sDV"]
            for ch in range(c.NCH):
                B.dma("scalar", "sPK", park[0, ch], xsl(R2, ch, 0, c.HALF),
                      waits=[("sDV", snap_dv)])
            B.fence()
            if "ffnB" in phases:
                ffn_half(R1, 1)
            snap_dv = cnt["sDV"]
            dump1 = []
            for ch in range(c.NCH):
                dump1.append(B.dma("scalar", "sPK", park[1, ch], xsl(R1, ch, 0, c.HALF),
                                   waits=[("sDV", snap_dv)]))
            snap = [(s, v) for s, v in cnt.items() if v > 0]
            for ch in range(c.NCH):
                B.dma("sync", "sLD", xsl(R1, ch, 0, c.HALF), park[0, ch], waits=snap)
            B.fence()
            if "ffnB" in phases:
                ffn_half(R1, 0)
            B.fence()

            snap_dv = cnt["sDV"]
            for ch in range(c.NCH):
                B.dma("scalar", "sPK", xout[ch * 128:(ch + 1) * 128, 0:c.HALF],
                      xsl(R1, ch, 0, c.HALF), waits=[("sDV", snap_dv)])
            for ch in range(c.NCH):
                B.dma("gpsimd", "sGL0", xout[ch * 128:(ch + 1) * 128, c.HALF:c.T],
                      park[1, ch], waits=[("sPK", max(dump1))])
            B.wait("scalar", [("sPK", cnt["sPK"])])
            B.wait("gpsimd", [("sGL0", cnt["sGL0"])])
            B.wait("sync", [("sPK", cnt["sPK"]), ("sGL0", cnt["sGL0"])])

        final_cnts = []
        for name in ENGINES:
            def run(eng, name=name):
                B.begin(name, eng)
                schedule()
                final_cnts.append(dict(B.cnt))
            getattr(block, name)(run)
        assert all(fc == final_cnts[0] for fc in final_cnts), (
            "schedule replay diverged across engines")

    return nc



def prep_shared(cfg, inputs):
    c = cfg

    def t(x):
        return np.ascontiguousarray(np.asarray(x, dtype=np.float32))

    gw = t(inputs["L0_ffn_gate_w"]).reshape(c.E * c.ICN, 128, c.NCH, 128)
    uw = t(inputs["L0_ffn_up_w"]).reshape(c.E * c.ICN, 128, c.NCH, 128)
    dw = t(inputs["L0_ffn_down_w"]).reshape(c.E * c.NCH, 128, c.ICN, 128)
    sg = t(inputs["L0_ffn_shared_gate_w"]).reshape(c.ISCN, 128, c.NCH, 128)
    su = t(inputs["L0_ffn_shared_up_w"]).reshape(c.ISCN, 128, c.NCH, 128)
    sd = t(inputs["L0_ffn_shared_down_w"]).reshape(c.NCH, 128, c.ISCN, 128)
    wq = t(inputs["L0_attn_Wq"]).reshape(c.DM, c.H * 64)[:, c.q_perm]
    wk = t(inputs["L0_attn_Wk"]).reshape(c.DM, c.HKV * 64)
    wv = t(inputs["L0_attn_Wv"]).reshape(c.DM, c.HKV * 64)
    wo = t(inputs["L0_attn_Wo"]).reshape(c.H * 64, c.DM)[c.q_perm, :]

    def oc_tile(w, noc):
        return np.ascontiguousarray(
            w.reshape(c.NCH, 128, noc, 128).transpose(2, 1, 0, 3)
            .reshape(noc, 128, c.NCH * 128))

    m = dict(
        gwf=np.ascontiguousarray(gw.transpose(0, 3, 2, 1).reshape(
            c.E * c.ICN, 128, c.NCH * 128)),
        uwf=np.ascontiguousarray(uw.transpose(0, 3, 2, 1).reshape(
            c.E * c.ICN, 128, c.NCH * 128)),
        dwf=np.ascontiguousarray(dw.transpose(0, 3, 2, 1).reshape(
            c.E * c.NCH, 128, c.ICN * 128)),
        sgw=np.ascontiguousarray(sg.transpose(0, 3, 2, 1).reshape(
            c.ISCN, 128, c.NCH * 128)),
        suw=np.ascontiguousarray(su.transpose(0, 3, 2, 1).reshape(
            c.ISCN, 128, c.NCH * 128)),
        sdw=np.ascontiguousarray(sd.transpose(0, 3, 2, 1).reshape(
            c.NCH, 128, c.ISCN * 128)),
        wq=oc_tile(wq, c.QCH),
        wk=oc_tile(wk, c.KVCH),
        wv=oc_tile(wv, c.KVCH),
        wo=np.ascontiguousarray(
            wo.reshape(c.QCH, 128, c.NCH, 128).transpose(2, 1, 0, 3)
            .reshape(c.NCH, 128, c.QCH * 128)),
        rw=np.ascontiguousarray(np.asarray(inputs["L0_ffn_router_w"], np.float32)),
        sew=np.ascontiguousarray(np.asarray(inputs["L0_ffn_se_gate_w"], np.float32).reshape(1, c.DM)),
        snk=np.ascontiguousarray(np.asarray(inputs["L0_attn_sinks"], np.float32).reshape(1, c.H)),
        bq=np.ascontiguousarray(np.asarray(inputs["L0_attn_bq"], np.float32).reshape(c.H * 64)[c.q_perm].reshape(c.QCH, 128).T),
        bkv=np.ascontiguousarray(np.concatenate([
            np.asarray(inputs["L0_attn_bk"], np.float32).reshape(c.KVCH, 128).T,
            np.asarray(inputs["L0_attn_bv"], np.float32).reshape(c.KVCH, 128).T], axis=1)),
        bo=np.ascontiguousarray(np.asarray(inputs["L0_attn_bo"], np.float32).reshape(c.NCH, 128).T),
    )
    return m


def prep_inputs(cfg, inputs, core):
    m = dict(prep_shared(cfg, inputs))
    x = np.asarray(inputs["L0_attn_hidden_states"][core], dtype=np.float32)
    m["xin"] = np.ascontiguousarray(x.T)
    return m


_PROG_CACHE = {}


def kernel(**inputs):
    cfg = CFG_FULL
    consts = make_consts(cfg)
    nc = build_program(cfg)
    ncores = 8
    shared = prep_shared(cfg, inputs)
    shared.update({k: np.ascontiguousarray(v) for k, v in consts.items()})
    in_maps = []
    for core in range(ncores):
        m = dict(shared)
        x = np.asarray(inputs["L0_attn_hidden_states"][core], dtype=np.float32)
        m["xin"] = np.ascontiguousarray(x.T)
        in_maps.append(m)
    res = run_bass_kernel_spmd(nc, in_maps, core_ids=list(range(ncores)))
    out = np.empty((ncores, cfg.T, cfg.DM), dtype=np.float32)
    for core in range(ncores):
        out[core] = res.results[core]["xout"].reshape(cfg.DM, cfg.T).T
    return out
